# version 06
# baseline (speedup 1.0000x reference)
import numpy as np
import ml_dtypes
import concourse.bass as bass
import concourse.mybir as mybir
from concourse.bass_utils import run_bass_kernel_spmd

BFNP = ml_dtypes.bfloat16
_LAST_Q = None
DEBUG_NO_PIPELINE = False
F32 = mybir.dt.float32
BF16 = mybir.dt.bfloat16
AF = mybir.ActivationFunctionType
ALU = mybir.AluOpType
AX = mybir.AxisListType

B, S, E, H, D = 2, 4096, 2048, 16, 128
NCORE = 8
NTOK = 1024
FC = 16
CH = 512
ICM = 11
ICS = 22
NEG = np.float32(-30000.0)
SCALE = float(np.float32(np.float32(2048.0) ** np.float32(-0.25)))
LN_EPS = float(np.float32(1e-5))
INV_E = float(np.float32(1.0) / np.float32(2048.0))
KHALF = 256


def _tile_wT(w):
    O, I = w.shape
    noc, nic = O // 128, I // 128
    t = w.reshape(noc, 128, nic, 128)
    t = np.transpose(t, (0, 3, 2, 1))
    return np.ascontiguousarray(t.reshape(noc, 128, nic * 128))


def _rope_consts():
    ar = np.arange(0, E, 2, dtype=np.float32)
    inv = np.power(np.float32(10000.0), -(ar / np.float32(E))).astype(np.float32)
    emb = np.repeat(inv, 2).astype(np.float32)
    arg = (np.float32(S) * emb).astype(np.float32)
    sin = np.sin(arg).astype(np.float32)
    cos = np.cos(arg).astype(np.float32)
    sinp = sin.copy()
    sinp[:E // 2] = -sinp[:E // 2]
    return sinp, cos


def _perm_mats():
    L = np.zeros((FC, 2, 128, 128), dtype=np.float32)
    for c in range(FC):
        s0 = 2 * c if c < 8 else 2 * (c - 8)
        for t in range(128):
            j = 128 * c + t
            p = 2 * j + 1 if j < E // 2 else 2 * (j - E // 2)
            s, r = p // 128, p % 128
            L[c, 0 if s == s0 else 1, r, t] = 1.0
    return L


UNITS = [(1, k) for k in range(16)] + [(0, k) for k in range(8)]
def _unit_masked(u):
    lq, kc2 = UNITS[u]
    return (lq == 1 and kc2 >= 8) or (lq == 0)
MASKED_IDX = {}
for _u in range(24):
    if _unit_masked(_u):
        MASKED_IDX[_u] = len(MASKED_IDX)
NMASK = len(MASKED_IDX)


def _masks_for(i):
    out = np.zeros((NMASK, 128, 1024), dtype=np.float32)
    q = np.arange(CH)
    kk = np.arange(128)
    for u, mi in MASKED_IDX.items():
        lq, kc2 = UNITS[u]
        g = (7 - i) if lq == 1 else i
        for ks in range(2):
            keyg = kc2 * KHALF + ks * 128 + kk[:, None]
            qg = g * CH + q[None, :]
            out[mi, :, ks * 512:(ks + 1) * 512] = np.where(keyg <= qg, 0.0, NEG)
    return out.astype(BFNP)


def _prep_host(inputs):
    f = lambda k: np.asarray(inputs[k], dtype=np.float32)
    Wqkv, bqkv = f("L0_attn_Wqkv"), f("L0_attn_bqkv")
    Wout, bout = f("L0_attn_Wout"), f("L0_attn_bout")
    gate = f("L0_ffn_gate_w")
    w1, w2, w3 = f("L0_ffn_w1"), f("L0_ffn_w2"), f("L0_ffn_w3")
    sw1, sw2, sw3 = f("L0_ffn_sw1"), f("L0_ffn_sw2"), f("L0_ffn_sw3")
    x = f("L0_attn_x")

    sh = {}
    qs = _tile_wT(Wqkv)
    order = list(range(16, 32)) + list(range(32, 48)) + list(range(0, 16))
    sh["Wqkv_s"] = np.ascontiguousarray(qs[order])
    sh["bqkv_sb"] = np.ascontiguousarray(bqkv.reshape(48, 128).T[:, order])
    sh["Wout_s"] = _tile_wT(Wout)
    sh["W1_s"] = np.ascontiguousarray(np.stack([_tile_wT(w1[e]) for e in range(8)]))
    sh["W2_s"] = np.ascontiguousarray(np.stack([_tile_wT(w2[e]) for e in range(8)]))
    sh["W3_s"] = np.ascontiguousarray(np.stack([_tile_wT(w3[e]) for e in range(8)]))
    sh["SW1_s"] = _tile_wT(sw1)
    sh["SW2_s"] = _tile_wT(sw2)
    sh["SW3_s"] = _tile_wT(sw3)
    sh["bout_sb"] = np.ascontiguousarray(bout.reshape(16, 128).T)
    g = gate.reshape(8, FC, 128)
    sh["gate_sb"] = np.ascontiguousarray(np.transpose(g, (2, 1, 0)).reshape(128, FC * 8))
    sinp, cos = _rope_consts()
    sh["cos_sb"] = np.ascontiguousarray(cos.reshape(FC, 128).T)
    sh["sinp_sb"] = np.ascontiguousarray(sinp.reshape(FC, 128).T)
    sh["L_sb"] = np.ascontiguousarray(
        np.transpose(_perm_mats(), (2, 0, 1, 3)).reshape(128, FC * 2 * 128))
    sh["ident"] = np.eye(128, dtype=np.float32)
    Emat = np.zeros((128, 256), dtype=np.float32)
    for hh in range(16):
        Emat[:, hh * 16 + hh] = 1.0
    sh["Emat"] = Emat
    sel16 = np.zeros((16, 2048), dtype=np.float32)
    for hh in range(16):
        sel16[hh, hh * 128:(hh + 1) * 128] = 1.0
    sh["sel16"] = sel16
    sel4 = np.zeros((4, 512), dtype=np.float32)
    for k in range(4):
        sel4[k, k * 128:(k + 1) * 128] = 1.0
    sh["sel4"] = sel4
    sel8 = np.zeros((8, 1024), dtype=np.float32)
    for k in range(8):
        sel8[k, k * 128:(k + 1) * 128] = 1.0
    sh["sel8"] = sel8
    sh["ones_col"] = np.ones((128, 1), dtype=np.float32)
    sh["ones_row"] = np.ones((1, 128), dtype=np.float32)
    sh["epsv"] = np.broadcast_to(
        np.array([LN_EPS, 1e-12, 0.0, 0.0], dtype=np.float32), (128, 4)).copy()

    pc = []
    for c in range(NCORE):
        b, i = c // 4, c % 4
        rows = np.concatenate(
            [x[b, i * CH:(i + 1) * CH], x[b, (7 - i) * CH:(8 - i) * CH]], axis=0)
        xfm = np.ascontiguousarray(rows.T).reshape(FC, 128, NTOK)
        pc.append({"x_fm": xfm, "masks": _masks_for(i)})
    return sh, pc


def _assemble_out(results):
    out = np.empty((B, S, E), dtype=np.float32)
    for c in range(NCORE):
        b, i = c // 4, c % 4
        ofm = np.asarray(results[c]["out_fm"]).reshape(E, NTOK)
        rows = ofm.T
        out[b, i * CH:(i + 1) * CH] = rows[:CH]
        out[b, (7 - i) * CH:(8 - i) * CH] = rows[CH:]
    return np.ascontiguousarray(out)



class Sem:
    __slots__ = ("h", "n")
    def __init__(self, h):
        self.h = h
        self.n = 0


class Prog:
    ENGINES = ("tensor", "vector", "scalar", "gpsimd", "sync")

    def __init__(self):
        self.q = {e: [] for e in self.ENGINES}

    def op(self, eng, fn, wait=(), inc=None, inc_amt=1):
        w = [(s, v) for (s, v) in wait if s is not None and v > 0]
        if inc is not None:
            inc.n += inc_amt
        def run(engine, _w=w, _fn=fn, _inc=inc, _amt=inc_amt):
            for s, v in _w:
                engine.wait_ge(s.h, v)
            inst = _fn(engine)
            if _inc is not None:
                inst.then_inc(_inc.h, _amt)
        self.q[eng].append(run)
        return inc.n if inc is not None else 0

    def t(self, fn, wait=(), inc=None):
        return self.op("tensor", fn, wait, inc, 1)

    def v(self, fn, wait=(), inc=None):
        return self.op("vector", fn, wait, inc, 1)

    def s(self, fn, wait=(), inc=None):
        return self.op("scalar", fn, wait, inc, 1)



def build_nc(taps=(), attn_units=None):
    from contextlib import ExitStack
    nc = bass.Bass("TRN2", target_bir_lowering=False)
    taps = set(taps)
    P = Prog()
    es = ExitStack()
    sb = lambda name, shape, dt: es.enter_context(nc.sbuf_tensor(name, list(shape), dt))

    dr = {}
    def din(name, shape, dt=F32):
        dr[name] = nc.dram_tensor(name, list(shape), dt, kind="ExternalInput")
    din("x_fm", (FC, 128, NTOK))
    din("masks", (NMASK, 128, 1024), BF16)
    din("Wqkv_s", (48, 128, 2048)); din("Wout_s", (16, 128, 2048))
    din("W1_s", (8, ICM, 128, 2048)); din("W2_s", (8, ICM, 128, 2048))
    din("W3_s", (8, FC, 128, ICM * 128))
    din("SW1_s", (ICS, 128, 2048)); din("SW2_s", (ICS, 128, 2048))
    din("SW3_s", (FC, 128, ICS * 128))
    din("bqkv_sb", (128, 48)); din("bout_sb", (128, 16))
    din("gate_sb", (128, FC * 8))
    din("cos_sb", (128, 16)); din("sinp_sb", (128, 16))
    din("L_sb", (128, FC * 2 * 128))
    din("ident", (128, 128)); din("Emat", (128, 256))
    din("sel16", (16, 2048))
    din("sel4", (4, 512)); din("sel8", (8, 1024))
    din("ones_col", (128, 1)); din("ones_row", (1, 128))
    din("epsv", (128, 4))

    out_fm = nc.dram_tensor("out_fm", [FC, 128, NTOK], F32, kind="ExternalOutput")
    cc_in = nc.dram_tensor("cc_in", [2 * E * NTOK], BF16, kind="Internal")
    cc_outk = nc.dram_tensor("cc_outk", [4, E * NTOK], BF16, kind="Internal")
    cc_outv = nc.dram_tensor("cc_outv", [4, E * NTOK], BF16, kind="Internal")
    x1_hbm = nc.dram_tensor("x1_hbm", [FC, 128, NTOK], F32, kind="Internal")
    tap_dr = {}
    for tn in taps:
        if tn == "gpool":
            tap_dr[tn] = nc.dram_tensor("tap_gpool", [128, 64], F32,
                                        kind="ExternalOutput")
        elif tn == "rsum":
            tap_dr[tn] = nc.dram_tensor("tap_rsum", [4, 512], F32,
                                        kind="ExternalOutput")
        else:
            tap_dr[tn] = nc.dram_tensor("tap_" + tn, [128, FC * NTOK], BF16,
                                        kind="ExternalOutput")

    AR_N = 70656
    AR = sb("AR", (128, AR_N), BF16)
    R0, R1, R2, POOL = 0, 16384, 32768, 49152
    MREG, PTREG, SQREG, ROWB = 65536, 66560, 68608, 69632
    def A(off, n):
        return AR[:, off:off + n]

    wbuf = [sb("wb0", (128, ICS * 128), BF16), sb("wb1", (128, ICS * 128), BF16),
            sb("wb2", (128, 2048), BF16), sb("wb3", (128, 2048), BF16),
            sb("wb4", (128, 2048), BF16), sb("wb5", (128, 2048), BF16)]
    xt = [sb(f"xt{k}", (128, NTOK), F32) for k in range(2)]
    ot = [sb(f"ot{k}", (128, NTOK), F32) for k in range(2)]
    t1s = [sb(f"t1s{k}", (128, 512), F32) for k in range(2)]
    rows = sb("rows", (1, 2048), F32)
    rs16 = sb("rs16", (16, 512), F32)
    rbsb = sb("rbsb", (128, 1024), F32)
    pp = sb("pp", (128, 8), F32)
    gsc = sb("gsc", (128, 24), F32)
    gpool = sb("gpool", (128, 64), F32)

    c_bqkv = sb("c_bqkv", (128, 48), F32)
    c_bout = sb("c_bout", (128, 16), F32)
    c_cos = sb("c_cos", (128, 16), F32)
    c_sinp = sb("c_sinp", (128, 16), F32)
    c_eps = sb("c_eps", (128, 4), F32)
    c_orow = sb("c_orow", (1, 128), F32)
    c_gate = sb("c_gate", (128, FC * 8), BF16)
    c_ident = sb("c_ident", (128, 128), BF16)
    c_E = sb("c_E", (128, 256), BF16)
    c_sel16 = sb("c_sel16", (16, 2048), BF16)
    c_ocol = sb("c_ocol", (128, 1), BF16)
    c_sel8 = sb("c_sel8", (8, 1024), BF16)

    psALL = es.enter_context(nc.psum_tensor("psALL", (128, 4096), F32))
    psA = psALL[:, 0:1024]
    psB = psALL[:, 1024:2048]
    psC = psALL[:, 2048:3072]
    psD = psALL[:, 3072:4096]
    PB = [
        (psA[:, 0:512], psA[:, 512:1024]),
        (psB[:, 0:512], psB[:, 512:1024]),
        (psC[:, 0:512], psC[:, 512:1024]),
        (psD[:, 0:512], psD[:, 512:1024]),
    ]

    mksem = lambda name: Sem(es.enter_context(nc.semaphore(name)))
    sT, sV, sS = mksem("sT"), mksem("sV"), mksem("sS")
    swl = [mksem(f"swl{k}") for k in range(6)]
    sxl = [mksem(f"sxl{k}") for k in range(2)]
    sml = [mksem(f"sml{k}") for k in range(2)]
    skl = [mksem(f"skl{k}") for k in range(2)]
    svl = [mksem(f"svl{k}") for k in range(2)]
    sStg = mksem("sStg")
    sCC = mksem("sCC")
    sCC2 = mksem("sCC2")
    sXw = mksem("sXw")
    sOut = mksem("sOut")
    sCg = mksem("sCg")
    sCs = mksem("sCs")

    PAIRF = [(None, 0)] * 4

    wfree = [(None, 0)] * 6
    wrr = [0]
    wrr_big = [0]

    def w_load(ap_fn, nic):
        if nic > 16:
            k = wrr_big[0] % 2
            wrr_big[0] += 1
        else:
            k = 2 + wrr[0] % 4
            wrr[0] += 1
        P.op("gpsimd",
             lambda e, k=k, n=nic, f=ap_fn: e.dma_start(out=wbuf[k][:, :n * 128], in_=f()),
             wait=[wfree[k]], inc=swl[k], inc_amt=16)
        return k, swl[k].n

    def mm_block(k, wtok, nic, rhs_fn, pi, extra_wait=()):
        for ic in range(nic):
            for th in range(2):
                wt = ([(swl[k], wtok), PAIRF[pi]] + list(extra_wait)
                      if (ic == 0 and th == 0) else [])
                last = (ic == nic - 1 and th == 1)
                P.t(lambda e, k=k, ic=ic, th=th, pi=pi, f=(ic == 0), l=(ic == nic - 1):
                    e.matmul(PB[pi][th], wbuf[k][:, ic * 128:(ic + 1) * 128],
                             rhs_fn(ic, th), start=f, stop=l),
                    wait=wt, inc=(sT if last else None))
        wfree[k] = (sT, sT.n)
        return sT.n

    MEAN = rows[0:1, 0:1024]
    TMP = rows[0:1, 1024:2048]
    T2 = rbsb[0:1, 0:1024]

    def ln_rows(wait_tok):
        P.v(lambda e: e.tensor_scalar(MEAN, psD[0:1, 0:1024], INV_E, None, ALU.mult),
            wait=[wait_tok])
        P.v(lambda e: e.tensor_scalar(rows[0:1, 1024:1536], PB[2][0][0:1, :], INV_E,
                                      None, ALU.mult))
        P.v(lambda e: e.tensor_scalar(rows[0:1, 1536:2048], PB[2][1][0:1, :], INV_E,
                                      None, ALU.mult))
        P.v(lambda e: e.drain())
        P.v(lambda e: e.tensor_tensor(T2, MEAN, MEAN, ALU.mult))
        P.v(lambda e: e.drain())
        P.v(lambda e: e.tensor_tensor(TMP, TMP, T2, ALU.subtract), inc=sV)
        vt = sV.n
        P.s(lambda e: e.activation(T2, TMP, AF.Sqrt, bias=c_eps[0:1, 0:1]),
            wait=[(sV, vt)], inc=sS)
        st = sS.n
        P.v(lambda e: e.reciprocal(TMP, T2), wait=[(sS, st)])
        P.v(lambda e: e.drain())
        P.v(lambda e: e.tensor_tensor(MEAN, MEAN, TMP, ALU.mult), inc=sV)
        PAIRF[2] = (sV, sV.n)
        PAIRF[3] = (sV, sV.n)
        return sV.n

    def ln_bcast(rows_tok, pir, pim):
        for th in range(2):
            P.t(lambda e, th=th, pir=pir: e.matmul(
                PB[pir][th], c_orow[0:1, :], TMP[0:1, th * 512:(th + 1) * 512],
                start=True, stop=True),
                wait=([(sV, rows_tok), PAIRF[pir], PAIRF[pim]] if th == 0 else []))
            P.t(lambda e, th=th, pim=pim: e.matmul(
                PB[pim][th], c_orow[0:1, :], MEAN[0:1, th * 512:(th + 1) * 512],
                start=True, stop=True),
                inc=(sT if th == 1 else None))
        return sT.n

    def tap(name, off, n):
        if name in tap_dr:
            P.op("sync", lambda e, name=name, off=off, n=n: e.dma_start(
                out=tap_dr[name].ap()[:, 0:n], in_=A(off, n)),
                wait=[(sV, sV.n), (sS, sS.n), (sT, sT.n)], inc=sOut, inc_amt=16)

    for name, buf in [("bqkv_sb", c_bqkv), ("bout_sb", c_bout), ("cos_sb", c_cos),
                      ("sinp_sb", c_sinp), ("epsv", c_eps), ("ones_row", c_orow)]:
        P.op("sync", lambda e, b=buf, n=name: e.dma_start(out=b[:, :], in_=dr[n].ap()),
             inc=sCs, inc_amt=16)
    for name, buf in [("gate_sb", c_gate), ("ident", c_ident), ("Emat", c_E),
                      ("ones_col", c_ocol), ("sel8", c_sel8), ("sel16", c_sel16)]:
        P.op("gpsimd", lambda e, b=buf, n=name: e.dma_start(out=b[:, :], in_=dr[n].ap()),
             inc=sCg, inc_amt=16)
    P.op("gpsimd", lambda e: e.dma_start(out=A(POOL, FC * 2 * 128), in_=dr["L_sb"].ap()),
         inc=sCg, inc_amt=16)
    CSTS = (sCs, sCs.n)
    CSTG = (sCg, sCg.n)

    xb_off, h_off = R1, R0
    xt_free = [[CSTS], [CSTS]]
    sq_free = [(None, 0), (None, 0)]
    for ic in range(FC):
        sl = ic % 2
        P.op("sync", lambda e, sl=sl, ic=ic: e.dma_start(
            out=xt[sl][:, :], in_=dr["x_fm"].ap()[ic]),
            wait=xt_free[sl], inc=sxl[sl], inc_amt=16)
        ltok = sxl[sl].n
        P.v(lambda e, ic=ic, sl=sl: e.tensor_copy(
            A(xb_off + ic * NTOK, NTOK), xt[sl][:, :]),
            wait=[(sxl[sl], ltok)], inc=sV)
        vtok = sV.n
        P.s(lambda e, sl=sl, ic=ic: e.activation(
            A(SQREG + 0 * NTOK, NTOK), xt[sl][:, :], AF.Square),
            wait=[(sxl[sl], ltok), sq_free[ic % 2]], inc=sS)
        stok = sS.n
        xt_free[sl] = [(sV, vtok), (sS, stok)]
        for th in range(2):
            P.t(lambda e, ic=ic, th=th: e.matmul(
                psD[0:1, th * 512:(th + 1) * 512], c_ocol[:, :],
                A(xb_off + ic * NTOK + th * 512, 512),
                start=(ic == 0), stop=(ic == FC - 1)),
                wait=([(sV, vtok), CSTG] if th == 0 else []))
            P.t(lambda e, ic=ic, th=th: e.matmul(
                PB[2][th][0:1, :], c_ocol[:, :],
                A(SQREG + 0 * NTOK + th * 512, 512),
                start=(ic == 0), stop=(ic == FC - 1)),
                wait=([(sS, stok)] if th == 0 else []),
                inc=(sT if th == 1 else None))
        sq_free[ic % 2] = (sT, sT.n)
    stats_tok = sT.n
    rtok = ln_rows((sT, stats_tok))
    btok = ln_bcast(rtok, 1, 0)
    for ic in range(FC):
        for th in range(2):
            P.v(lambda e, ic=ic, th=th: e.tensor_tensor(
                A(h_off + ic * NTOK + th * 512, 512),
                A(xb_off + ic * NTOK + th * 512, 512), PB[1][th], ALU.mult),
                wait=([(sT, btok)] if (ic == 0 and th == 0) else []))
            P.v(lambda e, ic=ic, th=th: e.tensor_tensor(
                A(h_off + ic * NTOK + th * 512, 512),
                A(h_off + ic * NTOK + th * 512, 512), PB[0][th], ALU.subtract),
                inc=(sV if (ic == FC - 1 and th == 1) else None))
    h_tok = sV.n
    PAIRF[0] = (sV, h_tok)
    PAIRF[1] = (sV, h_tok)
    PAIRF[2] = (sV, rtok)
    PAIRF[3] = (sV, rtok)
    tap("h", h_off, FC * NTOK)

    def qkv_blocks(idx0, evac_fn):
        for n in range(16):
            k, wtok = w_load(lambda o=idx0 + n: dr["Wqkv_s"].ap()[o], FC)
            pi = n % 2
            mm_block(k, wtok, FC,
                     lambda ic, th: A(h_off + ic * NTOK + th * 512, 512), pi)
            evac_fn(n, pi, sT.n)
            PAIRF[pi] = (sV, sV.n)

    def evac_qk(dst_off, bias0):
        def ev(oc, pi, mmtok):
            for th in range(2):
                P.v(lambda e, oc=oc, th=th, pi=pi: e.tensor_scalar(
                    A(dst_off + oc * NTOK + th * 512, 512), PB[pi][th],
                    c_bqkv[:, bias0 + oc:bias0 + oc + 1], None, ALU.add),
                    wait=([(sT, mmtok)] if th == 0 else []),
                    inc=(sV if th == 1 else None))
        return ev

    def rope_norm(raw_off, rot_off, sq_init, extra_rot_wait=()):
        stok_sq = [sq_init, sq_init]
        first_rot = [True]
        for c in range(FC):
            s0 = 2 * c if c < 8 else 2 * (c - 8)
            pi = c % 2
            for th in range(2):
                for j in range(2):
                    wt = [PAIRF[pi]] if (th == 0 and j == 0) else []
                    P.t(lambda e, c=c, th=th, j=j, s=(s0 + j), pi=pi: e.matmul(
                        PB[pi][th], AR[:, POOL + (c * 2 + j) * 128:
                                       POOL + (c * 2 + j + 1) * 128],
                        A(raw_off + s * NTOK + th * 512, 512),
                        start=(j == 0), stop=(j == 1)),
                        wait=wt, inc=(sT if (th == 1 and j == 1) else None))
            xrtok = sT.n
            for th in range(2):
                P.v(lambda e, c=c, th=th: e.tensor_scalar(
                    t1s[th][:, :], A(raw_off + c * NTOK + th * 512, 512),
                    c_cos[:, c:c + 1], None, ALU.mult),
                    wait=([(sT, xrtok)] +
                          (list(extra_rot_wait) if first_rot[0] else []))
                    if th == 0 else [])
                first_rot[0] = False
                P.v(lambda e, c=c, th=th, pi=pi: e.scalar_tensor_tensor(
                    A(rot_off + c * NTOK + th * 512, 512), PB[pi][th],
                    c_sinp[:, c:c + 1], t1s[th][:, :], ALU.mult, ALU.add),
                    inc=(sV if th == 1 else None))
            PAIRF[pi] = (sV, sV.n)
            rottok = sV.n
            P.s(lambda e, c=c: e.activation(
                A(SQREG + 0 * NTOK, NTOK), A(rot_off + c * NTOK, NTOK),
                AF.Square),
                wait=[(sV, rottok), stok_sq[c % 2]], inc=sS)
            sqtok = sS.n
            for th in range(2):
                P.t(lambda e, c=c, th=th: e.matmul(
                    psD[0:1, th * 512:(th + 1) * 512], c_ocol[:, :],
                    A(SQREG + 0 * NTOK + th * 512, 512),
                    start=(c == 0), stop=(c == FC - 1)),
                    wait=([(sS, sqtok)] + ([PAIRF[3]] if c == 0 else []))
                    if th == 0 else [],
                    inc=(sT if th == 1 else None))
            stok_sq[c % 2] = (sT, sT.n)
        n2tok = sT.n
        P.s(lambda e: e.activation(T2, psD[0:1, 0:1024], AF.Sqrt),
            wait=[(sT, n2tok)], inc=sS)
        P.v(lambda e: e.tensor_scalar(T2, T2, 1e-12, None, ALU.max),
            wait=[(sS, sS.n)])
        P.v(lambda e: e.drain())
        P.v(lambda e: e.reciprocal(TMP, T2))
        P.v(lambda e: e.drain())
        P.v(lambda e: e.tensor_scalar(TMP, TMP, SCALE, None, ALU.mult), inc=sV)
        rntok = sV.n
        PAIRF[3] = (sS, sS.n)
        for th in range(2):
            P.t(lambda e, th=th: e.matmul(
                PB[2][th], c_orow[0:1, :], TMP[0:1, th * 512:(th + 1) * 512],
                start=True, stop=True),
                wait=([(sV, rntok), PAIRF[2]] if th == 0 else []),
                inc=(sT if th == 1 else None))
        bctok = sT.n
        for c in range(FC):
            for th in range(2):
                P.v(lambda e, c=c, th=th: e.tensor_tensor(
                    A(rot_off + c * NTOK + th * 512, 512),
                    A(rot_off + c * NTOK + th * 512, 512), PB[2][th], ALU.mult),
                    wait=([(sT, bctok)] if (c == 0 and th == 0) else []),
                    inc=(sV if (c == FC - 1 and th == 1) else None))
        PAIRF[2] = (sV, sV.n)
        return sV.n

    kraw_off, krot_off = R2, R1
    qkv_blocks(0, evac_qk(kraw_off, 0))
    krot_tok = rope_norm(kraw_off, krot_off, (sT, stats_tok), extra_rot_wait=[CSTG])
    tap("krot", krot_off, FC * NTOK)
    cck = cc_in.ap()[0:E * NTOK].rearrange("(fc p t) -> fc p t", fc=FC, p=128)
    for g4 in range(4):
        P.op("sync", lambda e, g=g4: e.dma_start(
            out=cck[g * 4:(g + 1) * 4].rearrange("fc p t -> p fc t"),
            in_=A(krot_off + g * 4 * NTOK, 4 * NTOK).rearrange(
                "p (fc t) -> p fc t", t=NTOK)),
            wait=[(sV, krot_tok)], inc=sStg, inc_amt=16)
    kstg_tok = sStg.n
    P.op("gpsimd", lambda e: e.collective_compute(
        "AllGather", ALU.bypass,
        replica_groups=[[0, 1, 2, 3], [4, 5, 6, 7]],
        ins=[cc_in.ap()[0:E * NTOK]], outs=[cc_outk.ap()]),
        wait=[(sStg, kstg_tok)], inc=sCC, inc_amt=1)

    vtm_off = R1
    vsq_free = [(sT, sT.n), (sT, sT.n)]
    tp_free = [(sV, sV.n), (sV, sV.n)]
    first_vtm = [True]

    def evac_v(fv, pi, mmtok):
        sq = fv % 2
        for th in range(2):
            P.v(lambda e, fv=fv, th=th, pi=pi, sq=sq: e.tensor_scalar(
                A(SQREG + 0 * NTOK + th * 512, 512), PB[pi][th],
                c_bqkv[:, 16 + fv:16 + fv + 1], None, ALU.add),
                wait=([(sT, mmtok), vsq_free[sq]] if th == 0 else []),
                inc=(sV if th == 1 else None))
        evtok = sV.n
        for tc in range(8):
            pt = tc % 2
            P.t(lambda e, tc=tc, pt=pt, sq=sq: e.transpose(
                psC.bitcast(BF16)[:, pt * 1024:pt * 1024 + 128],
                A(SQREG + 0 * NTOK + tc * 128, 128),
                c_ident[:, :]),
                wait=([(sV, evtok)] if tc < 2 else []) + [tp_free[pt]],
                inc=sT)
            tttok = sT.n
            wt = [(sT, tttok)]
            if first_vtm[0]:
                wt.append((sStg, kstg_tok))
                first_vtm[0] = False
            P.v(lambda e, fv=fv, tc=tc, pt=pt: e.tensor_copy(
                A(vtm_off + tc * E + fv * 128, 128),
                psC.bitcast(BF16)[:, pt * 1024:pt * 1024 + 128]),
                wait=wt, inc=sV)
            tp_free[pt] = (sV, sV.n)
        vsq_free[sq] = (sV, sV.n)

    PAIRF[2] = (sV, sV.n)
    qkv_blocks(16, evac_v)
    PAIRF[2] = (sV, sV.n)
    vtm_tok = sV.n
    tap("vtm", vtm_off, 8 * E)
    ccv = cc_in.ap()[E * NTOK:2 * E * NTOK].rearrange("(tc p f) -> tc p f", tc=8, p=128)
    for g2 in range(2):
        P.op("sync", lambda e, g=g2: e.dma_start(
            out=ccv[g * 4:(g + 1) * 4].rearrange("tc p f -> p tc f"),
            in_=A(vtm_off + g * 4 * E, 4 * E).rearrange("p (tc f) -> p tc f", f=E)),
            wait=[(sV, vtm_tok)], inc=sStg, inc_amt=16)
    stg_tok = sStg.n
    P.op("gpsimd", lambda e: e.collective_compute(
        "AllGather", ALU.bypass,
        replica_groups=[[0, 1, 2, 3], [4, 5, 6, 7]],
        ins=[cc_in.ap()[E * NTOK:2 * E * NTOK]], outs=[cc_outv.ap()]),
        wait=[(sStg, stg_tok)], inc=sCC2, inc_amt=1)

    qraw_off, qrot_off = R1, R2
    first_q = [True]

    def evac_q(oc, pi, mmtok):
        for th in range(2):
            wt = [(sT, mmtok)]
            if first_q[0]:
                wt.append((sStg, stg_tok))
                first_q[0] = False
            P.v(lambda e, oc=oc, th=th, pi=pi: e.tensor_scalar(
                A(qraw_off + oc * NTOK + th * 512, 512), PB[pi][th],
                c_bqkv[:, 32 + oc:32 + oc + 1], None, ALU.add),
                wait=(wt if th == 0 else []),
                inc=(sV if th == 1 else None))

    qkv_blocks(32, evac_q)
    qkv_mm_tok = sT.n
    qrot_tok = rope_norm(qraw_off, qrot_off, vsq_free[0])
    tap("qrot", qrot_off, FC * NTOK)

    attn_off = R1
    KS = [POOL, POOL + 4096]
    VS = [POOL + 8192, POOL + 12288]
    MS = [MREG, MREG]
    SPA = [psALL[:, 0:1024], psALL[:, 2560:3584]]
    AVB2 = [psALL[:, 1024:1536], psALL[:, 1536:2048]]
    RSB16 = psALL[0:16, 3584:4096]
    AVT_SB = A(R0, 16384).bitcast(F32)

    ck = cc_outk.ap().rearrange("r (fc p t) -> r fc p t", fc=FC, p=128)

    kslot_free = [(sV, qrot_tok), (sV, qrot_tok)]
    vslot_free = [(sV, qrot_tok), (sV, qrot_tok)]
    mslot_free = [CSTG, CSTG]
    pT_free = [(sV, qrot_tok), (sV, qrot_tok)]
    sc_free = [(sV, qrot_tok), (sV, qrot_tok)]
    avb_free = [(sV, qrot_tok), (sV, qrot_tok)]
    avt_guard = [(sT, qkv_mm_tok)]
    rs_free = [(sS, sS.n)]
    rb_free = [(sS, sS.n), (sS, sS.n)]
    uh = [0]
    kld, vld, mld = [0], [0], [0]
    attn_guard = [(sV, sV.n), (sS, sS.n)]
    pend = [None]

    def flushp():
        if pend[0] is not None:
            pend[0]()
            pend[0] = None

    for lq in (1, 0):
        nk0 = 16 if lq == 1 else 8
        kc2_list = list(range(nk0)) if attn_units is None else attn_units.get(lq, [])
        if not kc2_list:
            continue
        nk = len(kc2_list)
        for kpos, kc2 in enumerate(kc2_list):
            u = kc2 if lq == 1 else 16 + kc2
            masked = u in MASKED_IDX
            j, hhf = kc2 // 2, kc2 % 2
            r = min(j, 7 - j)
            off = (0 if j < 4 else 512) + hhf * KHALF
            tcbase = (0 if j < 4 else 4) + hhf * 2
            ksl = kld[0] % 2
            P.op("scalar", lambda e, r=r, off=off, ksl=ksl: e.dma_start(
                out=AR[:, KS[ksl]:KS[ksl] + FC * KHALF].rearrange(
                    "p (fc t) -> p fc t", t=KHALF),
                in_=ck[r, :, :, off:off + KHALF].rearrange(
                    "fc p t -> p fc t")),
                wait=[kslot_free[ksl], (sCC, 1)], inc=skl[ksl], inc_amt=16)
            ktok = skl[ksl].n
            kld[0] += 1
            vsl = vld[0] % 2
            P.op("scalar", lambda e, r=r, tb=tcbase, vsl=vsl: e.dma_start(
                out=AR[:, VS[vsl]:VS[vsl] + 2 * E].rearrange(
                    "p (tc f) -> p tc f", f=E),
                in_=cc_outv.ap()[r:r + 1,
                                 tb * 128 * E:(tb + 2) * 128 * E].rearrange(
                    "o (tc p f) -> (o p) tc f", tc=2, p=128)),
                wait=[vslot_free[vsl], (sCC2, 1)], inc=svl[vsl], inc_amt=16)
            vtok = svl[vsl].n
            vld[0] += 1
            mtok = 0
            if masked:
                msl = 0
                mi = MASKED_IDX[u]
                P.op("sync", lambda e, mi=mi, msl=msl: e.dma_start(
                    out=A(MS[msl], 1024), in_=dr["masks"].ap()[mi]),
                    wait=[mslot_free[msl]], inc=sml[msl], inc_amt=16)
                mtok = sml[msl].n
                mld[0] += 1
            first_kc = (kpos == 0)
            for h in range(16):
                i_uh = uh[0]
                uh[0] += 1
                sp = i_uh % 2
                ptb = PTREG + sp * 1024
                wt = [sc_free[sp]]
                if h == 0:
                    wt.append((skl[ksl], ktok))
                if i_uh == 0:
                    wt += attn_guard + [(sV, qrot_tok)]
                for ksub in range(2):
                    P.t(lambda e, h=h, ksub=ksub, ksl=ksl, lq=lq, sp=sp,
                        st=(not masked): e.matmul(
                        SPA[sp][:, ksub * 512:(ksub + 1) * 512],
                        AR[:, KS[ksl] + h * KHALF + ksub * 128:
                           KS[ksl] + h * KHALF + ksub * 128 + 128],
                        A(qrot_off + h * NTOK + lq * 512, 512),
                        start=True, stop=st),
                        wait=(wt if ksub == 0 else []),
                        inc=(sT if (ksub == 1 and not masked) else None))
                if masked:
                    for ksub in range(2):
                        P.t(lambda e, ksub=ksub, msl=msl, sp=sp: e.matmul(
                            SPA[sp][:, ksub * 512:(ksub + 1) * 512],
                            c_ident[:, :], A(MS[msl] + ksub * 512, 512),
                            start=False, stop=True),
                            wait=([(sml[msl], mtok)] if (ksub == 0 and h == 0)
                                  else []),
                            inc=(sT if ksub == 1 else None))
                sctok = sT.n
                P.s(lambda e, ptb=ptb, sp=sp: e.activation(
                    A(ptb, 1024), SPA[sp][:, 0:1024], AF.Exp),
                    wait=[(sT, sctok), pT_free[sp]], inc=sS)
                exptok = sS.n
                sc_free[sp] = (sS, exptok)

                def mk_avE(h=h, sp=sp, ptb=ptb, vsl=vsl, vtok=vtok,
                           exptok=exptok, first_kc=first_kc, kpos=kpos):
                    def em():
                        ab = h % 2
                        wt2 = [(sS, exptok), avb_free[ab]]
                        if h == 0:
                            wt2.append((svl[vsl], vtok))
                            wt2.append(rs_free[0])
                        for ksub in range(2):
                            P.t(lambda e, h=h, ksub=ksub, ab=ab, vsl=vsl,
                                ptb=ptb: e.matmul(
                                    AVB2[ab][:, 0:512],
                                    AR[:, VS[vsl] + ksub * E + h * 128:
                                       VS[vsl] + ksub * E + h * 128 + 128],
                                    A(ptb + ksub * 512, 512),
                                    start=(ksub == 0), stop=(ksub == 1)),
                                wait=(wt2 if ksub == 0 else []))
                        for ksub in range(2):
                            P.t(lambda e, h=h, ksub=ksub, ptb=ptb,
                                f=(h == 0 and ksub == 0),
                                l=(h == 15 and ksub == 1): e.matmul(
                                    RSB16, c_E[:, h * 16:(h + 1) * 16],
                                    A(ptb + ksub * 512, 512),
                                    start=f, stop=l),
                                inc=(sT if ksub == 1 else None))
                        avtok = sT.n
                        pT_free[sp] = (sT, avtok)
                        if h == 15:
                            vslot_free[vsl] = (sT, avtok)
                        ew = [(sT, avtok)]
                        if kpos == 0 and h < 2:
                            ew += avt_guard
                        if kpos == 0:
                            P.v(lambda e, h=h, ab=ab: e.tensor_copy(
                                AVT_SB[:, h * 512:(h + 1) * 512], AVB2[ab][:, 0:512]),
                                wait=ew, inc=sV)
                        else:
                            P.v(lambda e, h=h, ab=ab: e.tensor_tensor(
                                AVT_SB[:, h * 512:(h + 1) * 512], AVB2[ab][:, 0:512],
                                AVT_SB[:, h * 512:(h + 1) * 512], ALU.add),
                                wait=ew, inc=sV)
                        avb_free[ab] = (sV, sV.n)
                    return em
                flushp()
                pend[0] = mk_avE()
            kslot_free[ksl] = (sT, sT.n)
            if masked:
                mslot_free[msl] = (sT, sT.n)
            flushp()
            lastE = sT.n
            if kpos == 0:
                P.v(lambda e: e.tensor_copy(rs16[:, :], RSB16),
                    wait=[(sT, lastE)], inc=sV)
            else:
                P.v(lambda e: e.tensor_tensor(rs16[:, :], RSB16, rs16[:, :],
                                              ALU.add),
                    wait=[(sT, lastE)], inc=sV)
            rs_free[0] = (sV, sV.n)
        P.v(lambda e: e.drain())
        P.v(lambda e: e.reciprocal(rs16[:, :], rs16[:, :]))
        P.v(lambda e: e.drain())
        P.v(lambda e: e.tensor_copy(AR[0:16, ROWB:ROWB + 512], rs16[:, :]), inc=sV)
        rectok = sV.n
        rs_free[0] = (sV, rectok)
        cp_toks = {}
        for h in range(16):
            scb = h % 2
            bw = ([sc_free[0], sc_free[1], (sV, rectok)] if h < 2
                  else [(sS, cp_toks[h - 2])])
            P.t(lambda e, h=h, scb=scb: e.matmul(
                psA[:, scb * 512:(scb + 1) * 512],
                c_sel16[:, h * 128:(h + 1) * 128], AR[0:16, ROWB:ROWB + 512],
                start=True, stop=True),
                wait=bw, inc=sT)
            bctok = sT.n
            P.s(lambda e, scb=scb: e.copy(
                rbsb[:, scb * 512:(scb + 1) * 512],
                psA[:, scb * 512:(scb + 1) * 512]),
                wait=[(sT, bctok), rb_free[scb]], inc=sS)
            cp_toks[h] = sS.n
            P.v(lambda e, h=h, lq=lq, scb=scb: e.tensor_tensor(
                A(attn_off + h * NTOK + lq * 512, 512),
                AVT_SB[:, h * 512:(h + 1) * 512],
                rbsb[:, scb * 512:(scb + 1) * 512], ALU.mult),
                wait=[(sS, cp_toks[h])], inc=sV)
            rb_free[scb] = (sV, sV.n)
        sc_free[0] = (sS, sS.n)
        sc_free[1] = (sS, sS.n)
        avb_free = [(sV, sV.n), (sV, sV.n)]
    attn_mm_tok = sT.n
    attn_tok = sV.n
    PAIRF[0] = (sS, sS.n)
    PAIRF[1] = (sV, attn_tok)
    PAIRF[2] = (sV, attn_tok)
    PAIRF[3] = (sV, attn_tok)
    tap("attn", attn_off, FC * NTOK)

    aob_off = R2
    first_aob = [True]
    sq_free2 = [(sT, sT.n), (sT, sT.n)]
    sum_guard = [(sV, attn_tok)]
    for oc in range(FC):
        k, wtok = w_load(lambda o=oc: dr["Wout_s"].ap()[o], FC)
        pi = oc % 2
        mm_block(k, wtok, FC,
                 lambda ic, th: A(attn_off + ic * NTOK + th * 512, 512), pi,
                 extra_wait=([(sV, attn_tok)] if oc == 0 else []))
        mmtok = sT.n
        for th in range(2):
            wt = [(sT, mmtok)]
            if first_aob[0]:
                wt.append((sT, attn_mm_tok))
                first_aob[0] = False
            P.v(lambda e, oc=oc, th=th, pi=pi: e.tensor_scalar(
                A(aob_off + oc * NTOK + th * 512, 512), PB[pi][th],
                c_bout[:, oc:oc + 1], None, ALU.add),
                wait=(wt if th == 0 else []),
                inc=(sV if th == 1 else None))
        evtok = sV.n
        PAIRF[pi] = (sV, evtok)
        P.s(lambda e, oc=oc: e.activation(
            A(SQREG + 0 * NTOK, NTOK), A(aob_off + oc * NTOK, NTOK),
            AF.Square),
            wait=[(sV, evtok), sq_free2[oc % 2]], inc=sS)
        sqtok = sS.n
        for th in range(2):
            P.t(lambda e, oc=oc, th=th: e.matmul(
                psD[0:1, th * 512:(th + 1) * 512], c_ocol[:, :],
                A(aob_off + oc * NTOK + th * 512, 512),
                start=(oc == 0), stop=(oc == FC - 1)),
                wait=([(sV, evtok)] + (sum_guard if oc == 0 else []))
                if th == 0 else [])
            P.t(lambda e, oc=oc, th=th: e.matmul(
                PB[2][th][0:1, :], c_ocol[:, :],
                A(SQREG + 0 * NTOK + th * 512, 512),
                start=(oc == 0), stop=(oc == FC - 1)),
                wait=([(sS, sqtok)] + ([PAIRF[2]] if oc == 0 else []))
                if th == 0 else [],
                inc=(sT if th == 1 else None))
        sq_free2[oc % 2] = (sT, sT.n)
    wout_mm_tok = sT.n
    rtok = ln_rows((sT, wout_mm_tok))
    btok = ln_bcast(rtok, 0, 1)
    x1b_off = R1
    first_x1b = [True]
    ot_free = [[(sV, sV.n)], [(sV, sV.n)]]
    sq_free3 = [(sT, sT.n), (sT, sT.n)]
    for ic in range(FC):
        sl = ic % 2
        P.op("sync", lambda e, sl=sl, ic=ic: e.dma_start(
            out=xt[sl][:, :], in_=dr["x_fm"].ap()[ic]),
            wait=xt_free[sl], inc=sxl[sl], inc_amt=16)
        ltok = sxl[sl].n
        for th in range(2):
            P.v(lambda e, ic=ic, th=th: e.tensor_tensor(
                t1s[th][:, :], A(aob_off + ic * NTOK + th * 512, 512), PB[0][th],
                ALU.mult),
                wait=([(sT, btok)] if (ic == 0 and th == 0) else []))
            P.v(lambda e, th=th: e.tensor_tensor(
                t1s[th][:, :], t1s[th][:, :], PB[1][th], ALU.subtract))
            wt = [(sxl[sl], ltok)] + (ot_free[sl] if th == 0 else [])
            P.v(lambda e, sl=sl, th=th: e.tensor_tensor(
                ot[sl][:, th * 512:(th + 1) * 512],
                xt[sl][:, th * 512:(th + 1) * 512],
                t1s[th][:, :], ALU.add),
                wait=wt, inc=(sV if th == 1 else None))
        x1tok = sV.n
        xt_free[sl] = [(sV, x1tok)]
        P.op("sync", lambda e, sl=sl, ic=ic: e.dma_start(
            out=x1_hbm.ap()[ic], in_=ot[sl][:, :]),
            wait=[(sV, x1tok)], inc=sXw, inc_amt=16)
        xwtok = sXw.n
        wt = [(sV, x1tok)]
        if first_x1b[0]:
            wt.append((sT, wout_mm_tok))
            first_x1b[0] = False
        P.v(lambda e, ic=ic, sl=sl: e.tensor_copy(
            A(x1b_off + ic * NTOK, NTOK), ot[sl][:, :]), wait=wt, inc=sV)
        cptok = sV.n
        P.s(lambda e, ic=ic, sl=sl: e.activation(
            A(SQREG + 0 * NTOK, NTOK), ot[sl][:, :], AF.Square),
            wait=[(sV, x1tok), sq_free3[ic % 2]], inc=sS)
        sqtok = sS.n
        ot_free[sl] = [(sXw, xwtok), (sV, cptok), (sS, sqtok)]
        for th in range(2):
            P.t(lambda e, ic=ic, th=th: e.matmul(
                psD[0:1, th * 512:(th + 1) * 512], c_ocol[:, :],
                A(x1b_off + ic * NTOK + th * 512, 512),
                start=(ic == 0), stop=(ic == FC - 1)),
                wait=([(sV, cptok)] + ([(sV, rtok)] if ic == 0 else []))
                if th == 0 else [])
            P.t(lambda e, ic=ic, th=th: e.matmul(
                PB[2][th][0:1, :], c_ocol[:, :],
                A(SQREG + 0 * NTOK + th * 512, 512),
                start=(ic == 0), stop=(ic == FC - 1)),
                wait=([(sS, sqtok)] + ([(sV, rtok)] if ic == 0 else []))
                if th == 0 else [],
                inc=(sT if th == 1 else None))
        sq_free3[ic % 2] = (sT, sT.n)
    ln2_stats_tok = sT.n
    sandwich_v_tok = sV.n
    tap("x1b", x1b_off, FC * NTOK)
    rtok = ln_rows((sT, ln2_stats_tok))
    btok = ln_bcast(rtok, 0, 1)
    h2_off = R0
    first_h2 = [True]
    for ic in range(FC):
        for th in range(2):
            wt = []
            if first_h2[0]:
                wt = [(sT, btok), (sT, qkv_mm_tok), (sV, sandwich_v_tok)]
                first_h2[0] = False
            P.v(lambda e, ic=ic, th=th: e.tensor_tensor(
                A(h2_off + ic * NTOK + th * 512, 512),
                A(x1b_off + ic * NTOK + th * 512, 512), PB[0][th], ALU.mult),
                wait=wt)
            P.v(lambda e, ic=ic, th=th: e.tensor_tensor(
                A(h2_off + ic * NTOK + th * 512, 512),
                A(h2_off + ic * NTOK + th * 512, 512), PB[1][th], ALU.subtract),
                inc=(sV if (ic == FC - 1 and th == 1) else None))
    h2_tok = sV.n
    PAIRF[0] = (sV, h2_tok)
    PAIRF[1] = (sV, h2_tok)
    tap("h2", h2_off, FC * NTOK)

    gfree = [PAIRF[3], PAIRF[3]]
    gate_defer = []
    cb_last = [0]
    for tc in range(8):
        gb = tc % 2
        for ic in range(FC):
            P.t(lambda e, tc=tc, ic=ic, gb=gb: e.matmul(
                psD[:, gb * 512:gb * 512 + 8],
                A(h2_off + ic * NTOK + tc * 128, 128),
                c_gate[:, ic * 8:(ic + 1) * 8],
                start=(ic == 0), stop=(ic == FC - 1)),
                wait=([gfree[gb], (sV, h2_tok), CSTG] if ic == 0 else []),
                inc=(sT if ic == FC - 1 else None))
        gmtok = sT.n
        L = psD[:, gb * 512:gb * 512 + 8]
        P.v(lambda e, L=L: e.reduce_max(pp[:, 0:1], L, AX.X), wait=[(sT, gmtok)])
        P.v(lambda e: e.drain())
        P.v(lambda e: e.tensor_scalar(pp[:, 1:2], pp[:, 0:1], -1.0, None, ALU.mult),
            inc=sV)
        vt = sV.n
        P.s(lambda e, L=L: e.activation(gsc[:, 0:8], L, AF.Exp, bias=pp[:, 1:2]),
            wait=[(sV, vt)], inc=sS)
        st = sS.n
        gfree[gb] = (sS, st)
        P.v(lambda e: e.reduce_sum(pp[:, 2:3], gsc[:, 0:8], AX.X), wait=[(sS, st)])
        P.v(lambda e: e.drain())
        P.v(lambda e: e.reciprocal(pp[:, 3:4], pp[:, 2:3]))
        P.v(lambda e: e.drain())
        P.v(lambda e, tc=tc: e.tensor_scalar(
            gpool[:, tc * 8:(tc + 1) * 8], gsc[:, 0:8], pp[:, 3:4], None, ALU.mult))
        P.v(lambda e: e.drain())
        P.v(lambda e, tc=tc: e.reduce_max(pp[:, 4:5], gpool[:, tc * 8:(tc + 1) * 8],
                                          AX.X))
        P.v(lambda e: e.drain())
        P.v(lambda e, tc=tc: e.tensor_scalar(
            gsc[:, 8:16], gpool[:, tc * 8:(tc + 1) * 8], pp[:, 4:5], None, ALU.is_ge))
        P.v(lambda e: e.drain())
        P.v(lambda e, tc=tc: e.scalar_tensor_tensor(
            gsc[:, 16:24], gsc[:, 8:16], -2.0, gpool[:, tc * 8:(tc + 1) * 8],
            ALU.mult, ALU.add))
        P.v(lambda e: e.drain())
        P.v(lambda e: e.reduce_max(pp[:, 5:6], gsc[:, 16:24], AX.X))
        P.v(lambda e: e.drain())
        P.v(lambda e, tc=tc: e.tensor_scalar(
            gsc[:, 8:16], gpool[:, tc * 8:(tc + 1) * 8], pp[:, 5:6], None, ALU.is_ge))
        P.v(lambda e: e.drain())
        P.v(lambda e, tc=tc: e.tensor_tensor(
            gpool[:, tc * 8:(tc + 1) * 8], gpool[:, tc * 8:(tc + 1) * 8],
            gsc[:, 8:16], ALU.mult))
        P.v(lambda e: e.drain())
        P.v(lambda e, tc=tc: e.tensor_copy(
            AR[:, PTREG + tc * 8:PTREG + (tc + 1) * 8],
            gpool[:, tc * 8:(tc + 1) * 8]),
            inc=sV)
        cbtok = sV.n
        def mk_tp(tc=tc, gb=gb, cbtok=cbtok):
            def em():
                P.t(lambda e, tc=tc, gb=gb: e.transpose(
                    psD.bitcast(BF16)[0:8, gb * 1024:gb * 1024 + 128],
                    AR[:, PTREG + tc * 8:PTREG + (tc + 1) * 8], c_ident[:, :]),
                    wait=[(sV, cb_last[0]), tfree[gb]], inc=sT)
                tttok = sT.n
                P.v(lambda e, tc=tc, gb=gb: e.tensor_copy(
                    AR[0:8, ROWB + tc * 128:ROWB + (tc + 1) * 128],
                    psD.bitcast(BF16)[0:8, gb * 1024:gb * 1024 + 128]),
                    wait=[(sT, tttok)], inc=sV)
                tfree[gb] = (sV, sV.n)
            return em
        gate_defer.append(mk_tp())
    gate_exp_tok = sS.n
    tfree = [(sS, gate_exp_tok), (sS, gate_exp_tok)]
    cb_last[0] = sV.n

    def hs_off(ic):
        return (R2 + ic * NTOK) if ic < 16 else (POOL + (ic - 16) * NTOK)

    y_off = R1
    mreg_free = (sT, attn_mm_tok)
    first_hs = [True]
    comb_tok = None
    for oc in range(ICS):
        if oc == 3:
            for f in gate_defer:
                f()
            comb_tok = sV.n
            PAIRF[3] = (sV, comb_tok)
            if "gpool" in tap_dr:
                P.op("sync", lambda e: e.dma_start(
                    out=tap_dr["gpool"].ap(), in_=gpool[:, :]),
                    wait=[(sV, comb_tok)], inc=sOut, inc_amt=16)
            if "comb" in tap_dr:
                P.op("sync", lambda e: e.dma_start(
                    out=tap_dr["comb"].ap()[0:8, 0:1024],
                    in_=AR[0:8, ROWB:ROWB + 1024]),
                    wait=[(sV, comb_tok)], inc=sOut, inc_amt=16)
        k1, wtok1 = w_load(lambda o=oc: dr["SW1_s"].ap()[o], FC)
        mm_block(k1, wtok1, FC,
                 lambda ic, th: A(h2_off + ic * NTOK + th * 512, 512), 0,
                 extra_wait=([(sV, h2_tok)] if oc == 0 else []))
        g1tok = sT.n
        k2, wtok2 = w_load(lambda o=oc: dr["SW2_s"].ap()[o], FC)
        mm_block(k2, wtok2, FC,
                 lambda ic, th: A(h2_off + ic * NTOK + th * 512, 512), 1)
        g2tok = sT.n
        for th in range(2):
            P.s(lambda e, th=th: e.activation(
                A(MREG + th * 512, 512), PB[0][th], AF.Silu),
                wait=([(sT, g1tok), mreg_free] if th == 0 else []),
                inc=(sS if th == 1 else None))
        silu_tok = sS.n
        PAIRF[0] = (sS, silu_tok)
        for th in range(2):
            wt = [(sT, g2tok), (sS, silu_tok)]
            if first_hs[0]:
                wt.append((sT, attn_mm_tok))
                first_hs[0] = False
            P.v(lambda e, oc=oc, th=th: e.tensor_tensor(
                A(hs_off(oc) + th * 512, 512), PB[1][th],
                A(MREG + th * 512, 512), ALU.mult),
                wait=(wt if th == 0 else []),
                inc=(sV if th == 1 else None))
        PAIRF[1] = (sV, sV.n)
        mreg_free = (sV, sV.n)
    hs_tok = sV.n
    first_y = [True]
    for oc in range(FC):
        k3, wtok3 = w_load(lambda o=oc: dr["SW3_s"].ap()[o], ICS)
        pe = 2 + oc % 2
        mm_block(k3, wtok3, ICS, lambda ic, th: A(hs_off(ic) + th * 512, 512), pe,
                 extra_wait=([(sV, hs_tok)] if oc == 0 else []))
        shtok = sT.n
        for th in range(2):
            wt = [(sT, shtok)]
            if first_y[0]:
                wt.append((sV, hs_tok))
                first_y[0] = False
            P.v(lambda e, oc=oc, th=th, pe=pe: e.tensor_copy(
                A(y_off + oc * NTOK + th * 512, 512), PB[pe][th]),
                wait=(wt if th == 0 else []),
                inc=(sV if th == 1 else None))
        PAIRF[pe] = (sV, sV.n)
    sw3_tok = sT.n

    def cb_off(e):
        return (R2 + 11264 + e * NTOK) if e < 5 else (POOL + 11264 + (e - 5) * NTOK)

    first_cb = [True]
    for e_ in range(8):
        for th in range(2):
            wt = [PAIRF[1]] if th == 0 else []
            if first_cb[0]:
                wt.append((sT, sw3_tok))
                wt.append((sV, comb_tok))
                first_cb[0] = False
            P.t(lambda ee, e_=e_, th=th: ee.matmul(
                PB[1][th], c_sel8[:, e_ * 128:(e_ + 1) * 128],
                AR[0:8, ROWB + th * 512:ROWB + (th + 1) * 512],
                start=True, stop=True),
                wait=wt, inc=(sT if th == 1 else None))
        bctok = sT.n
        for th in range(2):
            P.v(lambda ee, e_=e_, th=th: ee.tensor_copy(
                AR[:, cb_off(e_) + th * 512:cb_off(e_) + (th + 1) * 512],
                PB[1][th]),
                wait=([(sT, bctok)] if th == 0 else []),
                inc=(sV if th == 1 else None))
        PAIRF[1] = (sV, sV.n)
    cb_tok = sV.n

    he_free = [(sT, sw3_tok), (sT, sw3_tok)]
    for e_ in range(8):
        heb = POOL if e_ % 2 else R2
        for oc in range(ICM):
            k1, wtok1 = w_load(lambda e2=e_, o=oc: dr["W1_s"].ap()[e2, o], FC)
            mm_block(k1, wtok1, FC,
                     lambda ic, th: A(h2_off + ic * NTOK + th * 512, 512), 0)
            g1tok = sT.n
            k2, wtok2 = w_load(lambda e2=e_, o=oc: dr["W2_s"].ap()[e2, o], FC)
            mm_block(k2, wtok2, FC,
                     lambda ic, th: A(h2_off + ic * NTOK + th * 512, 512), 1)
            g2tok = sT.n
            for th in range(2):
                P.s(lambda e, th=th: e.activation(
                    A(MREG + th * 512, 512), PB[0][th], AF.Silu),
                    wait=([(sT, g1tok), mreg_free] if th == 0 else []),
                    inc=(sS if th == 1 else None))
            silu_tok = sS.n
            PAIRF[0] = (sS, silu_tok)
            for th in range(2):
                P.v(lambda e, e_=e_, th=th: e.tensor_tensor(
                    A(MREG + 1024 + th * 512, 512), PB[1][th],
                    AR[:, cb_off(e_) + th * 512:cb_off(e_) + (th + 1) * 512],
                    ALU.mult),
                    wait=([(sT, g2tok)] if th == 0 else []))
                wt = [(sS, silu_tok)]
                if oc == 0 and th == 0:
                    wt.append(he_free[e_ % 2])
                P.v(lambda e, oc=oc, th=th, heb=heb: e.tensor_tensor(
                    A(heb + oc * NTOK + th * 512, 512),
                    A(MREG + 1024 + th * 512, 512),
                    A(MREG + th * 512, 512), ALU.mult),
                    wait=wt, inc=(sV if th == 1 else None))
            PAIRF[1] = (sV, sV.n)
            mreg_free = (sV, sV.n)
        he_tok = sV.n
        for oc in range(FC):
            k3, wtok3 = w_load(lambda e2=e_, o=oc: dr["W3_s"].ap()[e2, o], ICM)
            pe = 2 + oc % 2
            mm_block(k3, wtok3, ICM,
                     lambda ic, th, heb=heb: A(heb + ic * NTOK + th * 512, 512), pe,
                     extra_wait=([(sV, he_tok)] if oc == 0 else []))
            eotok = sT.n
            for th in range(2):
                P.v(lambda e, oc=oc, th=th, pe=pe: e.tensor_tensor(
                    A(y_off + oc * NTOK + th * 512, 512), PB[pe][th],
                    A(y_off + oc * NTOK + th * 512, 512), ALU.add),
                    wait=([(sT, eotok)] if th == 0 else []),
                    inc=(sV if th == 1 else None))
            PAIRF[pe] = (sV, sV.n)
        he_free[e_ % 2] = (sT, sT.n)
    y_tok = sV.n
    ffn_mm_tok = sT.n
    tap("y", y_off, FC * NTOK)

    sq_free4 = [(sT, sT.n), (sT, sT.n)]
    for ic in range(FC):
        P.s(lambda e, ic=ic: e.activation(
            A(SQREG + 0 * NTOK, NTOK), A(y_off + ic * NTOK, NTOK), AF.Square),
            wait=[(sV, y_tok), sq_free4[ic % 2]], inc=sS)
        sqtok = sS.n
        for th in range(2):
            P.t(lambda e, ic=ic, th=th: e.matmul(
                psD[0:1, th * 512:(th + 1) * 512], c_ocol[:, :],
                A(y_off + ic * NTOK + th * 512, 512),
                start=(ic == 0), stop=(ic == FC - 1)),
                wait=([(sV, y_tok), PAIRF[3]] if (ic == 0 and th == 0) else []))
            P.t(lambda e, ic=ic, th=th: e.matmul(
                PB[2][th][0:1, :], c_ocol[:, :],
                A(SQREG + 0 * NTOK + th * 512, 512),
                start=(ic == 0), stop=(ic == FC - 1)),
                wait=([(sS, sqtok)] + ([PAIRF[2]] if ic == 0 else []))
                if th == 0 else [],
                inc=(sT if th == 1 else None))
        sq_free4[ic % 2] = (sT, sT.n)
    ln3_stats_tok = sT.n
    rtok = ln_rows((sT, ln3_stats_tok))
    btok = ln_bcast(rtok, 0, 1)
    for ic in range(FC):
        sl = ic % 2
        P.op("scalar", lambda e, sl=sl, ic=ic: e.dma_start(
            out=xt[sl][:, :], in_=x1_hbm.ap()[ic]),
            wait=xt_free[sl] + ([(sXw, sXw.n)] if ic == 0 else []),
            inc=sxl[sl], inc_amt=16)
        ltok = sxl[sl].n
        for th in range(2):
            P.v(lambda e, ic=ic, th=th: e.tensor_tensor(
                t1s[th][:, :], A(y_off + ic * NTOK + th * 512, 512), PB[0][th],
                ALU.mult),
                wait=([(sT, btok)] if (ic == 0 and th == 0) else []))
            P.v(lambda e, th=th: e.tensor_tensor(
                t1s[th][:, :], t1s[th][:, :], PB[1][th], ALU.subtract))
            wt = [(sxl[sl], ltok)] + (ot_free[sl] if th == 0 else [])
            P.v(lambda e, sl=sl, th=th: e.tensor_tensor(
                ot[sl][:, th * 512:(th + 1) * 512],
                xt[sl][:, th * 512:(th + 1) * 512], t1s[th][:, :], ALU.add),
                wait=wt, inc=(sV if th == 1 else None))
        otok = sV.n
        xt_free[sl] = [(sV, otok)]
        P.op("sync", lambda e, sl=sl, ic=ic: e.dma_start(
            out=out_fm.ap()[ic], in_=ot[sl][:, :]),
            wait=[(sV, otok)], inc=sOut, inc_amt=16)
        ot_free[sl] = [(sOut, sOut.n)]

    fin = sOut.n
    P.op("sync", lambda e: e.nop(), wait=[(sOut, fin)])

    global _LAST_Q
    _LAST_Q = {e: len(P.q[e]) for e in P.ENGINES}
    with nc.Block() as block:
        @block.tensor
        def _(e):
            for f in P.q["tensor"]:
                f(e)

        @block.vector
        def _(e):
            for f in P.q["vector"]:
                f(e)

        @block.scalar
        def _(e):
            for f in P.q["scalar"]:
                f(e)

        @block.gpsimd
        def _(e):
            for f in P.q["gpsimd"]:
                f(e)

        @block.sync
        def _(e):
            for f in P.q["sync"]:
                f(e)

    es.close()
    return nc


def kernel(**inputs):
    sh, pc = _prep_host(inputs)
    nc = build_nc()
    in_maps = [dict(sh, **pc[c]) for c in range(NCORE)]
    res = run_bass_kernel_spmd(nc, in_maps, core_ids=list(range(NCORE)))
    return _assemble_out(res.results)


# version 11
# speedup vs baseline: 1.0656x; 1.0656x over previous
import numpy as np
import ml_dtypes
import concourse.bass as bass
import concourse.mybir as mybir
from concourse.bass_utils import run_bass_kernel_spmd

BFNP = ml_dtypes.bfloat16
_LAST_Q = None
DEBUG_NO_PIPELINE = False
F32 = mybir.dt.float32
BF16 = mybir.dt.bfloat16
AF = mybir.ActivationFunctionType
ALU = mybir.AluOpType
AX = mybir.AxisListType

B, S, E, H, D = 2, 4096, 2048, 16, 128
NCORE = 8
NTOK = 1024
FC = 16
CH = 512
ICM = 11
ICS = 22
NEG = np.float32(-30000.0)
SCALE = float(np.float32(np.float32(2048.0) ** np.float32(-0.25)))
LN_EPS = float(np.float32(1e-5))
INV_E = float(np.float32(1.0) / np.float32(2048.0))
KHALF = 256


def _tile_wT(w):
    O, I = w.shape
    noc, nic = O // 128, I // 128
    t = w.reshape(noc, 128, nic, 128)
    t = np.transpose(t, (0, 3, 2, 1))
    return np.ascontiguousarray(t.reshape(noc, 128, nic * 128))


def _rope_consts():
    ar = np.arange(0, E, 2, dtype=np.float32)
    inv = np.power(np.float32(10000.0), -(ar / np.float32(E))).astype(np.float32)
    emb = np.repeat(inv, 2).astype(np.float32)
    arg = (np.float32(S) * emb).astype(np.float32)
    sin = np.sin(arg).astype(np.float32)
    cos = np.cos(arg).astype(np.float32)
    sinp = sin.copy()
    sinp[:E // 2] = -sinp[:E // 2]
    return sinp, cos


def _perm_mats():
    L = np.zeros((FC, 2, 128, 128), dtype=np.float32)
    for c in range(FC):
        s0 = 2 * c if c < 8 else 2 * (c - 8)
        for t in range(128):
            j = 128 * c + t
            p = 2 * j + 1 if j < E // 2 else 2 * (j - E // 2)
            s, r = p // 128, p % 128
            L[c, 0 if s == s0 else 1, r, t] = 1.0
    return L


UNITS = [(1, k) for k in range(16)] + [(0, k) for k in range(8)]
def _unit_masked(u):
    lq, kc2 = UNITS[u]
    return (lq == 1 and kc2 >= 8) or (lq == 0)
MASKED_IDX = {}
for _u in range(24):
    if _unit_masked(_u):
        MASKED_IDX[_u] = len(MASKED_IDX)
NMASK = len(MASKED_IDX)


def _masks_for(i):
    out = np.zeros((NMASK, 128, 1024), dtype=np.float32)
    q = np.arange(CH)
    kk = np.arange(128)
    for u, mi in MASKED_IDX.items():
        lq, kc2 = UNITS[u]
        g = (7 - i) if lq == 1 else i
        for ks in range(2):
            keyg = kc2 * KHALF + ks * 128 + kk[:, None]
            qg = g * CH + q[None, :]
            out[mi, :, ks * 512:(ks + 1) * 512] = np.where(keyg <= qg, 0.0, NEG)
    return out.astype(BFNP)


def _prep_host(inputs):
    f = lambda k: np.asarray(inputs[k], dtype=np.float32)
    Wqkv, bqkv = f("L0_attn_Wqkv"), f("L0_attn_bqkv")
    Wout, bout = f("L0_attn_Wout"), f("L0_attn_bout")
    gate = f("L0_ffn_gate_w")
    w1, w2, w3 = f("L0_ffn_w1"), f("L0_ffn_w2"), f("L0_ffn_w3")
    sw1, sw2, sw3 = f("L0_ffn_sw1"), f("L0_ffn_sw2"), f("L0_ffn_sw3")
    x = f("L0_attn_x")

    sh = {}
    qs = _tile_wT(Wqkv)
    order = list(range(16, 32)) + list(range(32, 48)) + list(range(0, 16))
    sh["Wqkv_s"] = np.ascontiguousarray(qs[order])
    sh["bqkv_sb"] = np.ascontiguousarray(bqkv.reshape(48, 128).T[:, order])
    sh["Wout_s"] = _tile_wT(Wout)
    sh["W1_s"] = np.ascontiguousarray(np.stack([_tile_wT(w1[e]) for e in range(8)]))
    sh["W2_s"] = np.ascontiguousarray(np.stack([_tile_wT(w2[e]) for e in range(8)]))
    sh["W3_s"] = np.ascontiguousarray(np.stack([_tile_wT(w3[e]) for e in range(8)]))
    sh["SW1_s"] = _tile_wT(sw1)
    sh["SW2_s"] = _tile_wT(sw2)
    sh["SW3_s"] = _tile_wT(sw3)
    sh["bout_sb"] = np.ascontiguousarray(bout.reshape(16, 128).T)
    g = gate.reshape(8, FC, 128)
    sh["gate_sb"] = np.ascontiguousarray(np.transpose(g, (2, 1, 0)).reshape(128, FC * 8))
    sinp, cos = _rope_consts()
    sh["cos_sb"] = np.ascontiguousarray(cos.reshape(FC, 128).T)
    sh["sinp_sb"] = np.ascontiguousarray(sinp.reshape(FC, 128).T)
    sh["L_sb"] = np.ascontiguousarray(
        np.transpose(_perm_mats(), (2, 0, 1, 3)).reshape(128, FC * 2 * 128))
    sh["ident"] = np.eye(128, dtype=np.float32)
    Emat = np.zeros((128, 256), dtype=np.float32)
    for hh in range(16):
        Emat[:, hh * 16 + hh] = 1.0
    sh["Emat"] = Emat
    sel16 = np.zeros((16, 2048), dtype=np.float32)
    for hh in range(16):
        sel16[hh, hh * 128:(hh + 1) * 128] = 1.0
    sh["sel16"] = sel16
    sel4 = np.zeros((4, 512), dtype=np.float32)
    for k in range(4):
        sel4[k, k * 128:(k + 1) * 128] = 1.0
    sh["sel4"] = sel4
    sel8 = np.zeros((8, 1024), dtype=np.float32)
    for k in range(8):
        sel8[k, k * 128:(k + 1) * 128] = 1.0
    sh["sel8"] = sel8
    sh["ones_col"] = np.ones((128, 1), dtype=np.float32)
    sh["ones_row"] = np.ones((1, 128), dtype=np.float32)
    sh["epsv"] = np.broadcast_to(
        np.array([LN_EPS, 1e-12, 0.0, 0.0], dtype=np.float32), (128, 4)).copy()

    pc = []
    for c in range(NCORE):
        b, i = c // 4, c % 4
        rows = np.concatenate(
            [x[b, i * CH:(i + 1) * CH], x[b, (7 - i) * CH:(8 - i) * CH]], axis=0)
        xfm = np.ascontiguousarray(rows.T).reshape(FC, 128, NTOK)
        pc.append({"x_fm": xfm, "masks": _masks_for(i)})
    return sh, pc


def _assemble_out(results):
    out = np.empty((B, S, E), dtype=np.float32)
    for c in range(NCORE):
        b, i = c // 4, c % 4
        ofm = np.asarray(results[c]["out_fm"]).reshape(E, NTOK)
        rows = ofm.T
        out[b, i * CH:(i + 1) * CH] = rows[:CH]
        out[b, (7 - i) * CH:(8 - i) * CH] = rows[CH:]
    return np.ascontiguousarray(out)



class Sem:
    __slots__ = ("h", "n")
    def __init__(self, h):
        self.h = h
        self.n = 0


class Prog:
    ENGINES = ("tensor", "vector", "scalar", "gpsimd", "sync")

    def __init__(self):
        self.q = {e: [] for e in self.ENGINES}

    def op(self, eng, fn, wait=(), inc=None, inc_amt=1):
        w = [(s, v) for (s, v) in wait if s is not None and v > 0]
        if inc is not None:
            inc.n += inc_amt
        def run(engine, _w=w, _fn=fn, _inc=inc, _amt=inc_amt):
            for s, v in _w:
                engine.wait_ge(s.h, v)
            inst = _fn(engine)
            if _inc is not None:
                inst.then_inc(_inc.h, _amt)
        self.q[eng].append(run)
        return inc.n if inc is not None else 0

    def t(self, fn, wait=(), inc=None):
        return self.op("tensor", fn, wait, inc, 1)

    def v(self, fn, wait=(), inc=None):
        return self.op("vector", fn, wait, inc, 1)

    def s(self, fn, wait=(), inc=None):
        return self.op("scalar", fn, wait, inc, 1)



def build_nc(taps=(), attn_units=None):
    from contextlib import ExitStack
    nc = bass.Bass("TRN2", target_bir_lowering=False)
    taps = set(taps)
    P = Prog()
    es = ExitStack()
    sb = lambda name, shape, dt: es.enter_context(nc.sbuf_tensor(name, list(shape), dt))

    dr = {}
    def din(name, shape, dt=F32):
        dr[name] = nc.dram_tensor(name, list(shape), dt, kind="ExternalInput")
    din("x_fm", (FC, 128, NTOK))
    din("masks", (NMASK, 128, 1024), BF16)
    din("Wqkv_s", (48, 128, 2048)); din("Wout_s", (16, 128, 2048))
    din("W1_s", (8, ICM, 128, 2048)); din("W2_s", (8, ICM, 128, 2048))
    din("W3_s", (8, FC, 128, ICM * 128))
    din("SW1_s", (ICS, 128, 2048)); din("SW2_s", (ICS, 128, 2048))
    din("SW3_s", (FC, 128, ICS * 128))
    din("bqkv_sb", (128, 48)); din("bout_sb", (128, 16))
    din("gate_sb", (128, FC * 8))
    din("cos_sb", (128, 16)); din("sinp_sb", (128, 16))
    din("L_sb", (128, FC * 2 * 128))
    din("ident", (128, 128)); din("Emat", (128, 256))
    din("sel16", (16, 2048))
    din("sel4", (4, 512)); din("sel8", (8, 1024))
    din("ones_col", (128, 1)); din("ones_row", (1, 128))
    din("epsv", (128, 4))

    out_fm = nc.dram_tensor("out_fm", [FC, 128, NTOK], F32, kind="ExternalOutput")
    cc_ink = nc.dram_tensor("cc_ink", [E * NTOK], BF16, kind="Internal")
    cc_inv = nc.dram_tensor("cc_inv", [E * NTOK], BF16, kind="Internal")
    cc_outk = nc.dram_tensor("cc_outk", [4, E * NTOK], BF16, kind="Internal")
    cc_outv = nc.dram_tensor("cc_outv", [4, E * NTOK], BF16, kind="Internal")
    x1_hbm = nc.dram_tensor("x1_hbm", [FC, 128, NTOK], F32, kind="Internal")
    tap_dr = {}
    for tn in taps:
        if tn == "gpool":
            tap_dr[tn] = nc.dram_tensor("tap_gpool", [128, 64], F32,
                                        kind="ExternalOutput")
        elif tn == "ccinv":
            tap_dr[tn] = nc.dram_tensor("tap_ccinv", [E * NTOK], BF16,
                                        kind="ExternalOutput")
        elif tn in ("cck", "ccv"):
            tap_dr[tn] = nc.dram_tensor("tap_" + tn, [4, E * NTOK], BF16,
                                        kind="ExternalOutput")
        elif tn == "rsum":
            tap_dr[tn] = nc.dram_tensor("tap_rsum", [4, 512], F32,
                                        kind="ExternalOutput")
        else:
            tap_dr[tn] = nc.dram_tensor("tap_" + tn, [128, FC * NTOK], BF16,
                                        kind="ExternalOutput")

    AR_N = 70656
    AR = sb("AR", (128, AR_N), BF16)
    R0, R1, R2, POOL = 0, 16384, 32768, 49152
    MREG, PTREG, SQREG, ROWB = 65536, 66560, 68608, 69632
    def A(off, n):
        return AR[:, off:off + n]

    wbuf = [sb("wb0", (128, ICS * 128), BF16), sb("wb1", (128, ICS * 128), BF16),
            sb("wb2", (128, 2048), BF16), sb("wb3", (128, 2048), BF16),
            sb("wb4", (128, 2048), BF16), sb("wb5", (128, 2048), BF16)]
    xt = [sb(f"xt{k}", (128, NTOK), F32) for k in range(2)]
    ot = [sb(f"ot{k}", (128, NTOK), F32) for k in range(2)]
    t1s = [sb(f"t1s{k}", (128, 512), F32) for k in range(2)]
    rows = sb("rows", (1, 2048), F32)
    rs16 = sb("rs16", (16, 512), F32)
    rbsb = sb("rbsb", (128, 1024), F32)
    pp = sb("pp", (128, 8), F32)
    gsc = sb("gsc", (128, 24), F32)
    gpool = sb("gpool", (128, 64), F32)

    c_bqkv = sb("c_bqkv", (128, 48), F32)
    c_bout = sb("c_bout", (128, 16), F32)
    c_cos = sb("c_cos", (128, 16), F32)
    c_sinp = sb("c_sinp", (128, 16), F32)
    c_eps = sb("c_eps", (128, 4), F32)
    c_orow = sb("c_orow", (1, 128), F32)
    c_gate = sb("c_gate", (128, FC * 8), BF16)
    c_ident = sb("c_ident", (128, 128), BF16)
    c_E = sb("c_E", (128, 256), BF16)
    c_sel16 = sb("c_sel16", (16, 2048), BF16)
    c_ocol = sb("c_ocol", (128, 1), BF16)
    c_sel8 = sb("c_sel8", (8, 1024), BF16)

    psALL = es.enter_context(nc.psum_tensor("psALL", (128, 4096), F32))
    psA = psALL[:, 0:1024]
    psB = psALL[:, 1024:2048]
    psC = psALL[:, 2048:3072]
    psD = psALL[:, 3072:4096]
    PB = [
        (psA[:, 0:512], psA[:, 512:1024]),
        (psB[:, 0:512], psB[:, 512:1024]),
        (psC[:, 0:512], psC[:, 512:1024]),
        (psD[:, 0:512], psD[:, 512:1024]),
    ]

    mksem = lambda name: Sem(es.enter_context(nc.semaphore(name)))
    sT, sV, sS = mksem("sT"), mksem("sV"), mksem("sS")
    swl = [mksem(f"swl{k}") for k in range(6)]
    sxl = [mksem(f"sxl{k}") for k in range(2)]
    sml = [mksem(f"sml{k}") for k in range(2)]
    skl = [mksem(f"skl{k}") for k in range(2)]
    svl = [mksem(f"svl{k}") for k in range(2)]
    sStg = mksem("sStg")
    sCC = mksem("sCC")
    sCC2 = mksem("sCC2")
    sXw = mksem("sXw")
    sOut = mksem("sOut")
    sCg = mksem("sCg")
    sCs = mksem("sCs")

    PAIRF = [(None, 0)] * 4

    wfree = [(None, 0)] * 6
    wrr = [0]
    wrr_big = [0]

    def w_load(ap_fn, nic):
        if nic > 16:
            k = wrr_big[0] % 2
            wrr_big[0] += 1
        else:
            k = 2 + wrr[0] % 4
            wrr[0] += 1
        P.op("gpsimd",
             lambda e, k=k, n=nic, f=ap_fn: e.dma_start(out=wbuf[k][:, :n * 128], in_=f()),
             wait=[wfree[k]], inc=swl[k], inc_amt=16)
        return k, swl[k].n

    def mm_block(k, wtok, nic, rhs_fn, pi, extra_wait=()):
        for ic in range(nic):
            for th in range(2):
                wt = ([(swl[k], wtok), PAIRF[pi]] + list(extra_wait)
                      if (ic == 0 and th == 0) else [])
                last = (ic == nic - 1 and th == 1)
                P.t(lambda e, k=k, ic=ic, th=th, pi=pi, f=(ic == 0), l=(ic == nic - 1):
                    e.matmul(PB[pi][th], wbuf[k][:, ic * 128:(ic + 1) * 128],
                             rhs_fn(ic, th), start=f, stop=l),
                    wait=wt, inc=(sT if last else None))
        wfree[k] = (sT, sT.n)
        return sT.n

    MEAN = rows[0:1, 0:1024]
    TMP = rows[0:1, 1024:2048]
    T2 = rbsb[0:1, 0:1024]

    def ln_rows(wait_tok):
        P.v(lambda e: e.tensor_scalar(MEAN, psD[0:1, 0:1024], INV_E, None, ALU.mult),
            wait=[wait_tok])
        P.v(lambda e: e.tensor_scalar(rows[0:1, 1024:1536], PB[2][0][0:1, :], INV_E,
                                      None, ALU.mult))
        P.v(lambda e: e.tensor_scalar(rows[0:1, 1536:2048], PB[2][1][0:1, :], INV_E,
                                      None, ALU.mult))
        P.v(lambda e: e.drain())
        P.v(lambda e: e.tensor_tensor(T2, MEAN, MEAN, ALU.mult))
        P.v(lambda e: e.drain())
        P.v(lambda e: e.tensor_tensor(TMP, TMP, T2, ALU.subtract), inc=sV)
        vt = sV.n
        P.s(lambda e: e.activation(T2, TMP, AF.Sqrt, bias=c_eps[0:1, 0:1]),
            wait=[(sV, vt)], inc=sS)
        st = sS.n
        P.v(lambda e: e.reciprocal(TMP, T2), wait=[(sS, st)])
        P.v(lambda e: e.drain())
        P.v(lambda e: e.tensor_tensor(MEAN, MEAN, TMP, ALU.mult), inc=sV)
        PAIRF[2] = (sV, sV.n)
        PAIRF[3] = (sV, sV.n)
        return sV.n

    def ln_bcast(rows_tok, pir, pim):
        for th in range(2):
            P.t(lambda e, th=th, pir=pir: e.matmul(
                PB[pir][th], c_orow[0:1, :], TMP[0:1, th * 512:(th + 1) * 512],
                start=True, stop=True),
                wait=([(sV, rows_tok), PAIRF[pir], PAIRF[pim]] if th == 0 else []))
            P.t(lambda e, th=th, pim=pim: e.matmul(
                PB[pim][th], c_orow[0:1, :], MEAN[0:1, th * 512:(th + 1) * 512],
                start=True, stop=True),
                inc=(sT if th == 1 else None))
        return sT.n

    def tap(name, off, n):
        if name in tap_dr:
            P.op("sync", lambda e, name=name, off=off, n=n: e.dma_start(
                out=tap_dr[name].ap()[:, 0:n], in_=A(off, n)),
                wait=[(sV, sV.n), (sS, sS.n), (sT, sT.n)], inc=sOut, inc_amt=16)

    for name, buf in [("bqkv_sb", c_bqkv), ("bout_sb", c_bout), ("cos_sb", c_cos),
                      ("sinp_sb", c_sinp), ("epsv", c_eps), ("ones_row", c_orow)]:
        P.op("sync", lambda e, b=buf, n=name: e.dma_start(out=b[:, :], in_=dr[n].ap()),
             inc=sCs, inc_amt=16)
    for name, buf in [("gate_sb", c_gate), ("ident", c_ident), ("Emat", c_E),
                      ("ones_col", c_ocol), ("sel8", c_sel8), ("sel16", c_sel16)]:
        P.op("gpsimd", lambda e, b=buf, n=name: e.dma_start(out=b[:, :], in_=dr[n].ap()),
             inc=sCg, inc_amt=16)
    P.op("gpsimd", lambda e: e.dma_start(out=A(POOL, FC * 2 * 128), in_=dr["L_sb"].ap()),
         inc=sCg, inc_amt=16)
    CSTS = (sCs, sCs.n)
    CSTG = (sCg, sCg.n)

    xb_off, h_off = R1, R0
    xt_free = [[CSTS], [CSTS]]
    sq_free = [(None, 0), (None, 0)]
    for ic in range(FC):
        sl = ic % 2
        P.op("sync", lambda e, sl=sl, ic=ic: e.dma_start(
            out=xt[sl][:, :], in_=dr["x_fm"].ap()[ic]),
            wait=xt_free[sl], inc=sxl[sl], inc_amt=16)
        ltok = sxl[sl].n
        P.v(lambda e, ic=ic, sl=sl: e.tensor_copy(
            A(xb_off + ic * NTOK, NTOK), xt[sl][:, :]),
            wait=[(sxl[sl], ltok)], inc=sV)
        vtok = sV.n
        P.s(lambda e, sl=sl, ic=ic: e.activation(
            A(SQREG + 0 * NTOK, NTOK), xt[sl][:, :], AF.Square),
            wait=[(sxl[sl], ltok), sq_free[ic % 2]], inc=sS)
        stok = sS.n
        xt_free[sl] = [(sV, vtok), (sS, stok)]
        for th in range(2):
            P.t(lambda e, ic=ic, th=th: e.matmul(
                psD[0:1, th * 512:(th + 1) * 512], c_ocol[:, :],
                A(xb_off + ic * NTOK + th * 512, 512),
                start=(ic == 0), stop=(ic == FC - 1)),
                wait=([(sV, vtok), CSTG] if th == 0 else []))
            P.t(lambda e, ic=ic, th=th: e.matmul(
                PB[2][th][0:1, :], c_ocol[:, :],
                A(SQREG + 0 * NTOK + th * 512, 512),
                start=(ic == 0), stop=(ic == FC - 1)),
                wait=([(sS, stok)] if th == 0 else []),
                inc=(sT if th == 1 else None))
        sq_free[ic % 2] = (sT, sT.n)
    stats_tok = sT.n
    rtok = ln_rows((sT, stats_tok))
    btok = ln_bcast(rtok, 1, 0)
    for ic in range(FC):
        for th in range(2):
            P.v(lambda e, ic=ic, th=th: e.tensor_tensor(
                A(h_off + ic * NTOK + th * 512, 512),
                A(xb_off + ic * NTOK + th * 512, 512), PB[1][th], ALU.mult),
                wait=([(sT, btok)] if (ic == 0 and th == 0) else []))
            P.v(lambda e, ic=ic, th=th: e.tensor_tensor(
                A(h_off + ic * NTOK + th * 512, 512),
                A(h_off + ic * NTOK + th * 512, 512), PB[0][th], ALU.subtract),
                inc=(sV if (ic == FC - 1 and th == 1) else None))
    h_tok = sV.n
    PAIRF[0] = (sV, h_tok)
    PAIRF[1] = (sV, h_tok)
    PAIRF[2] = (sV, rtok)
    PAIRF[3] = (sV, rtok)
    tap("h", h_off, FC * NTOK)

    def qkv_blocks(idx0, evac_fn):
        for n in range(16):
            k, wtok = w_load(lambda o=idx0 + n: dr["Wqkv_s"].ap()[o], FC)
            pi = n % 2
            mm_block(k, wtok, FC,
                     lambda ic, th: A(h_off + ic * NTOK + th * 512, 512), pi)
            evac_fn(n, pi, sT.n)
            PAIRF[pi] = (sV, sV.n)

    def evac_qk(dst_off, bias0):
        def ev(oc, pi, mmtok):
            for th in range(2):
                P.v(lambda e, oc=oc, th=th, pi=pi: e.tensor_scalar(
                    A(dst_off + oc * NTOK + th * 512, 512), PB[pi][th],
                    c_bqkv[:, bias0 + oc:bias0 + oc + 1], None, ALU.add),
                    wait=([(sT, mmtok)] if th == 0 else []),
                    inc=(sV if th == 1 else None))
        return ev

    def rope_norm(raw_off, rot_off, sq_init, extra_rot_wait=()):
        stok_sq = [sq_init, sq_init]
        first_rot = [True]
        for c in range(FC):
            s0 = 2 * c if c < 8 else 2 * (c - 8)
            pi = c % 2
            for th in range(2):
                for j in range(2):
                    wt = [PAIRF[pi]] if (th == 0 and j == 0) else []
                    P.t(lambda e, c=c, th=th, j=j, s=(s0 + j), pi=pi: e.matmul(
                        PB[pi][th], AR[:, POOL + (c * 2 + j) * 128:
                                       POOL + (c * 2 + j + 1) * 128],
                        A(raw_off + s * NTOK + th * 512, 512),
                        start=(j == 0), stop=(j == 1)),
                        wait=wt, inc=(sT if (th == 1 and j == 1) else None))
            xrtok = sT.n
            for th in range(2):
                P.v(lambda e, c=c, th=th: e.tensor_scalar(
                    t1s[th][:, :], A(raw_off + c * NTOK + th * 512, 512),
                    c_cos[:, c:c + 1], None, ALU.mult),
                    wait=([(sT, xrtok)] +
                          (list(extra_rot_wait) if first_rot[0] else []))
                    if th == 0 else [])
                first_rot[0] = False
                P.v(lambda e, c=c, th=th, pi=pi: e.scalar_tensor_tensor(
                    A(rot_off + c * NTOK + th * 512, 512), PB[pi][th],
                    c_sinp[:, c:c + 1], t1s[th][:, :], ALU.mult, ALU.add),
                    inc=(sV if th == 1 else None))
            PAIRF[pi] = (sV, sV.n)
            rottok = sV.n
            P.s(lambda e, c=c: e.activation(
                A(SQREG + 0 * NTOK, NTOK), A(rot_off + c * NTOK, NTOK),
                AF.Square),
                wait=[(sV, rottok), stok_sq[c % 2]], inc=sS)
            sqtok = sS.n
            for th in range(2):
                P.t(lambda e, c=c, th=th: e.matmul(
                    psD[0:1, th * 512:(th + 1) * 512], c_ocol[:, :],
                    A(SQREG + 0 * NTOK + th * 512, 512),
                    start=(c == 0), stop=(c == FC - 1)),
                    wait=([(sS, sqtok)] + ([PAIRF[3]] if c == 0 else []))
                    if th == 0 else [],
                    inc=(sT if th == 1 else None))
            stok_sq[c % 2] = (sT, sT.n)
        n2tok = sT.n
        P.s(lambda e: e.activation(T2, psD[0:1, 0:1024], AF.Sqrt),
            wait=[(sT, n2tok)], inc=sS)
        P.v(lambda e: e.tensor_scalar(T2, T2, 1e-12, None, ALU.max),
            wait=[(sS, sS.n)])
        P.v(lambda e: e.drain())
        P.v(lambda e: e.reciprocal(TMP, T2))
        P.v(lambda e: e.drain())
        P.v(lambda e: e.tensor_scalar(TMP, TMP, SCALE, None, ALU.mult), inc=sV)
        rntok = sV.n
        PAIRF[3] = (sS, sS.n)
        for th in range(2):
            P.t(lambda e, th=th: e.matmul(
                PB[2][th], c_orow[0:1, :], TMP[0:1, th * 512:(th + 1) * 512],
                start=True, stop=True),
                wait=([(sV, rntok), PAIRF[2]] if th == 0 else []),
                inc=(sT if th == 1 else None))
        bctok = sT.n
        for c in range(FC):
            for th in range(2):
                P.v(lambda e, c=c, th=th: e.tensor_tensor(
                    A(rot_off + c * NTOK + th * 512, 512),
                    A(rot_off + c * NTOK + th * 512, 512), PB[2][th], ALU.mult),
                    wait=([(sT, bctok)] if (c == 0 and th == 0) else []),
                    inc=(sV if (c == FC - 1 and th == 1) else None))
        PAIRF[2] = (sV, sV.n)
        return sV.n

    kraw_off, krot_off = R2, R1
    vtm_off = R2
    vev_off = POOL + 4096
    qkv_blocks(0, evac_qk(kraw_off, 0))
    krot_tok = rope_norm(kraw_off, krot_off, (sT, stats_tok), extra_rot_wait=[CSTG])
    k_stat_done = (sT, sT.n)
    tap("krot", krot_off, FC * NTOK)

    cck = cc_ink.ap().rearrange("(fc p t) -> fc p t", fc=FC, p=128)
    for g4 in range(4):
        P.op("sync", lambda e, g=g4: e.dma_start(
            out=cck[g * 4:(g + 1) * 4].rearrange("fc p t -> p fc t"),
            in_=A(krot_off + g * 4 * NTOK, 4 * NTOK).rearrange(
                "p (fc t) -> p fc t", t=NTOK)),
            wait=[(sV, krot_tok)], inc=sStg, inc_amt=16)
    kstg_tok = sStg.n
    P.op("gpsimd", lambda e: e.collective_compute(
        "AllGather", ALU.bypass,
        replica_groups=[[0, 1, 2, 3], [4, 5, 6, 7]],
        ins=[cc_ink.ap()], outs=[cc_outk.ap()]),
        wait=[(sStg, kstg_tok)], inc=sCC, inc_amt=1)

    vev_free = [(sS, sS.n)]
    tp_free = [(sV, krot_tok), (sV, krot_tok)]

    def v_block(n, pi):
        k2, wtok = w_load(lambda o=16 + n: dr["Wqkv_s"].ap()[o], FC)
        mm_block(k2, wtok, FC,
                 lambda ic, th: A(h_off + ic * NTOK + th * 512, 512), pi)
        mmtok = sT.n
        for th in range(2):
            P.s(lambda e, n=n, th=th, pi=pi: e.activation(
                A(vev_off + th * 512, 512), PB[pi][th], AF.Identity,
                bias=c_bqkv[:, 16 + n:17 + n]),
                wait=([(sT, mmtok), vev_free[0]] if th == 0 else []),
                inc=(sS if th == 1 else None))
        evtok = sS.n
        PAIRF[pi] = (sS, evtok)
        last_tt = 0
        for tc in range(8):
            pt = tc % 2
            P.t(lambda e, tc=tc, pt=pt: e.transpose(
                psC.bitcast(BF16)[:, pt * 1024:pt * 1024 + 128],
                A(vev_off + tc * 128, 128), c_ident[:, :]),
                wait=([(sS, evtok)] if tc < 2 else []) + [tp_free[pt]],
                inc=sT)
            tttok = sT.n
            last_tt = tttok
            P.s(lambda e, n=n, tc=tc, pt=pt: e.copy(
                A(vtm_off + tc * E + n * 128, 128),
                psC.bitcast(BF16)[:, pt * 1024:pt * 1024 + 128]),
                wait=[(sT, tttok)], inc=sS)
            tp_free[pt] = (sS, sS.n)
        vev_free[0] = (sT, last_tt)

    for n in range(FC):
        v_block(n, n % 2)
    PAIRF[2] = (sS, sS.n)
    vtm_s_tok = sS.n
    ccv = cc_inv.ap().rearrange("(tc p f) -> tc p f", tc=8, p=128)
    for g2 in range(2):
        P.op("sync", lambda e, g=g2: e.dma_start(
            out=ccv[g * 4:(g + 1) * 4].rearrange("tc p f -> p tc f"),
            in_=A(vtm_off + g * 4 * E, 4 * E).rearrange(
                "p (tc f) -> p tc f", f=E)),
            wait=[(sS, vtm_s_tok)], inc=sStg, inc_amt=16)
    vstg_tok = sStg.n
    if "ccinv" in tap_dr:
        P.op("sync", lambda e: e.dma_start(
            out=tap_dr["ccinv"].ap(), in_=cc_inv.ap()),
            wait=[(sStg, vstg_tok)], inc=sOut, inc_amt=16)
    P.op("gpsimd", lambda e: e.collective_compute(
        "AllGather", ALU.bypass,
        replica_groups=[[0, 1, 2, 3], [4, 5, 6, 7]],
        ins=[cc_inv.ap()], outs=[cc_outv.ap()]),
        wait=[(sStg, vstg_tok)], inc=sCC2, inc_amt=1)

    qraw_off, qrot_off = R1, R2
    first_q = [True]

    def evac_q(oc, pi, mmtok):
        for th in range(2):
            wt = [(sT, mmtok)]
            if first_q[0]:
                wt.append((sStg, kstg_tok))
                first_q[0] = False
            P.v(lambda e, oc=oc, th=th, pi=pi: e.tensor_scalar(
                A(qraw_off + oc * NTOK + th * 512, 512), PB[pi][th],
                c_bqkv[:, 32 + oc:32 + oc + 1], None, ALU.add),
                wait=(wt if th == 0 else []),
                inc=(sV if th == 1 else None))

    qkv_blocks(32, evac_q)
    qkv_mm_tok = sT.n
    qrot_tok = rope_norm(qraw_off, qrot_off, k_stat_done,
                         extra_rot_wait=[(sStg, vstg_tok)])
    tap("qrot", qrot_off, FC * NTOK)

    attn_off = R1
    KS = [POOL, POOL + 4096]
    VS = [POOL + 8192, POOL + 12288]
    MS = [MREG, MREG]
    SPA = [psALL[:, 0:1024], psALL[:, 2560:3584]]
    AVB2 = [psALL[:, 1024:1536], psALL[:, 1536:2048]]
    RSB16 = psALL[0:16, 3584:4096]
    AVT_SB = A(R0, 16384).bitcast(F32)

    ck = cc_outk.ap().rearrange("r (fc p t) -> r fc p t", fc=FC, p=128)

    kslot_free = [(sStg, vstg_tok), (sStg, vstg_tok)]
    vslot_free = [(sStg, vstg_tok), (sStg, vstg_tok)]
    pool_guard = [(sV, qrot_tok)]
    mslot_free = [CSTG, CSTG]
    pT_free = [(sV, qrot_tok), (sV, qrot_tok)]
    sc_free = [(sV, qrot_tok), (sV, qrot_tok)]
    avb_free = [(sV, qrot_tok), (sV, qrot_tok)]
    avt_guard = [(sT, qkv_mm_tok)]
    rs_free = [(sS, sS.n)]
    rb_free = [(sS, sS.n), (sS, sS.n)]
    uh = [0]
    kld, vld, mld = [0], [0], [0]
    attn_guard = [(sV, sV.n), (sS, sS.n)]
    pend = [None]

    def flushp():
        if pend[0] is not None:
            pend[0]()
            pend[0] = None

    for lq in (1, 0):
        nk0 = 16 if lq == 1 else 8
        kc2_list = list(range(nk0)) if attn_units is None else attn_units.get(lq, [])
        if not kc2_list:
            continue
        nk = len(kc2_list)
        for kpos, kc2 in enumerate(kc2_list):
            u = kc2 if lq == 1 else 16 + kc2
            masked = u in MASKED_IDX
            j, hhf = kc2 // 2, kc2 % 2
            r = min(j, 7 - j)
            off = (0 if j < 4 else 512) + hhf * KHALF
            tcbase = (0 if j < 4 else 4) + hhf * 2
            ksl = kld[0] % 2
            P.op("scalar", lambda e, r=r, off=off, ksl=ksl: e.dma_start(
                out=AR[:, KS[ksl]:KS[ksl] + FC * KHALF].rearrange(
                    "p (fc t) -> p fc t", t=KHALF),
                in_=ck[r, :, :, off:off + KHALF].rearrange(
                    "fc p t -> p fc t")),
                wait=[kslot_free[ksl], (sCC, 1)] +
                     (pool_guard if kld[0] < 2 else []),
                inc=skl[ksl], inc_amt=16)
            ktok = skl[ksl].n
            kld[0] += 1
            vsl = vld[0] % 2
            P.op("scalar", lambda e, r=r, tb=tcbase, vsl=vsl: e.dma_start(
                out=AR[:, VS[vsl]:VS[vsl] + 2 * E].rearrange(
                    "p (tc f) -> p tc f", f=E),
                in_=cc_outv.ap()[r:r + 1,
                                 tb * 128 * E:(tb + 2) * 128 * E].rearrange(
                    "o (tc p f) -> (o p) tc f", tc=2, p=128)),
                wait=[vslot_free[vsl], (sCC2, 1)] +
                     (pool_guard if vld[0] < 2 else []),
                inc=svl[vsl], inc_amt=16)
            vtok = svl[vsl].n
            vld[0] += 1
            mtok = 0
            if masked:
                msl = 0
                mi = MASKED_IDX[u]
                P.op("sync", lambda e, mi=mi, msl=msl: e.dma_start(
                    out=A(MS[msl], 1024), in_=dr["masks"].ap()[mi]),
                    wait=[mslot_free[msl]], inc=sml[msl], inc_amt=16)
                mtok = sml[msl].n
                mld[0] += 1
            first_kc = (kpos == 0)
            for h in range(16):
                i_uh = uh[0]
                uh[0] += 1
                sp = i_uh % 2
                ptb = PTREG + sp * 1024
                wt = [sc_free[sp]]
                if h == 0:
                    wt.append((skl[ksl], ktok))
                if i_uh == 0:
                    wt += attn_guard + [(sV, qrot_tok)]
                for ksub in range(2):
                    P.t(lambda e, h=h, ksub=ksub, ksl=ksl, lq=lq, sp=sp,
                        st=(not masked): e.matmul(
                        SPA[sp][:, ksub * 512:(ksub + 1) * 512],
                        AR[:, KS[ksl] + h * KHALF + ksub * 128:
                           KS[ksl] + h * KHALF + ksub * 128 + 128],
                        A(qrot_off + h * NTOK + lq * 512, 512),
                        start=True, stop=st),
                        wait=(wt if ksub == 0 else []),
                        inc=(sT if (ksub == 1 and not masked) else None))
                if masked:
                    for ksub in range(2):
                        P.t(lambda e, ksub=ksub, msl=msl, sp=sp: e.matmul(
                            SPA[sp][:, ksub * 512:(ksub + 1) * 512],
                            c_ident[:, :], A(MS[msl] + ksub * 512, 512),
                            start=False, stop=True),
                            wait=([(sml[msl], mtok)] if (ksub == 0 and h == 0)
                                  else []),
                            inc=(sT if ksub == 1 else None))
                sctok = sT.n
                P.s(lambda e, ptb=ptb, sp=sp: e.activation(
                    A(ptb, 1024), SPA[sp][:, 0:1024], AF.Exp),
                    wait=[(sT, sctok), pT_free[sp]], inc=sS)
                exptok = sS.n
                sc_free[sp] = (sS, exptok)

                def mk_avE(h=h, sp=sp, ptb=ptb, vsl=vsl, vtok=vtok,
                           exptok=exptok, first_kc=first_kc, kpos=kpos):
                    def em():
                        ab = h % 2
                        wt2 = [(sS, exptok), avb_free[ab]]
                        if h == 0:
                            wt2.append((svl[vsl], vtok))
                            wt2.append(rs_free[0])
                        for ksub in range(2):
                            P.t(lambda e, h=h, ksub=ksub, ab=ab, vsl=vsl,
                                ptb=ptb: e.matmul(
                                    AVB2[ab][:, 0:512],
                                    AR[:, VS[vsl] + ksub * E + h * 128:
                                       VS[vsl] + ksub * E + h * 128 + 128],
                                    A(ptb + ksub * 512, 512),
                                    start=(ksub == 0), stop=(ksub == 1)),
                                wait=(wt2 if ksub == 0 else []))
                        for ksub in range(2):
                            P.t(lambda e, h=h, ksub=ksub, ptb=ptb,
                                f=(h == 0 and ksub == 0),
                                l=(h == 15 and ksub == 1): e.matmul(
                                    RSB16, c_E[:, h * 16:(h + 1) * 16],
                                    A(ptb + ksub * 512, 512),
                                    start=f, stop=l),
                                inc=(sT if ksub == 1 else None))
                        avtok = sT.n
                        pT_free[sp] = (sT, avtok)
                        if h == 15:
                            vslot_free[vsl] = (sT, avtok)
                        ew = [(sT, avtok)]
                        if kpos == 0 and h < 2:
                            ew += avt_guard
                        if kpos == 0:
                            P.v(lambda e, h=h, ab=ab: e.tensor_copy(
                                AVT_SB[:, h * 512:(h + 1) * 512], AVB2[ab][:, 0:512]),
                                wait=ew, inc=sV)
                        else:
                            P.v(lambda e, h=h, ab=ab: e.tensor_tensor(
                                AVT_SB[:, h * 512:(h + 1) * 512], AVB2[ab][:, 0:512],
                                AVT_SB[:, h * 512:(h + 1) * 512], ALU.add),
                                wait=ew, inc=sV)
                        avb_free[ab] = (sV, sV.n)
                    return em
                flushp()
                pend[0] = mk_avE()
            kslot_free[ksl] = (sT, sT.n)
            if masked:
                mslot_free[msl] = (sT, sT.n)
            flushp()
            lastE = sT.n
            if kpos == 0:
                P.v(lambda e: e.tensor_copy(rs16[:, :], RSB16),
                    wait=[(sT, lastE)], inc=sV)
            else:
                P.v(lambda e: e.tensor_tensor(rs16[:, :], RSB16, rs16[:, :],
                                              ALU.add),
                    wait=[(sT, lastE)], inc=sV)
            rs_free[0] = (sV, sV.n)
        P.v(lambda e: e.drain())
        P.v(lambda e: e.reciprocal(rs16[:, :], rs16[:, :]))
        P.v(lambda e: e.drain())
        P.v(lambda e: e.tensor_copy(AR[0:16, ROWB:ROWB + 512], rs16[:, :]), inc=sV)
        rectok = sV.n
        rs_free[0] = (sV, rectok)
        cp_toks = {}
        for h in range(16):
            scb = h % 2
            bw = ([sc_free[0], sc_free[1], (sV, rectok)] if h < 2
                  else [(sS, cp_toks[h - 2])])
            P.t(lambda e, h=h, scb=scb: e.matmul(
                psA[:, scb * 512:(scb + 1) * 512],
                c_sel16[:, h * 128:(h + 1) * 128], AR[0:16, ROWB:ROWB + 512],
                start=True, stop=True),
                wait=bw, inc=sT)
            bctok = sT.n
            P.s(lambda e, scb=scb: e.copy(
                rbsb[:, scb * 512:(scb + 1) * 512],
                psA[:, scb * 512:(scb + 1) * 512]),
                wait=[(sT, bctok), rb_free[scb]], inc=sS)
            cp_toks[h] = sS.n
            P.v(lambda e, h=h, lq=lq, scb=scb: e.tensor_tensor(
                A(attn_off + h * NTOK + lq * 512, 512),
                AVT_SB[:, h * 512:(h + 1) * 512],
                rbsb[:, scb * 512:(scb + 1) * 512], ALU.mult),
                wait=[(sS, cp_toks[h])], inc=sV)
            rb_free[scb] = (sV, sV.n)
        sc_free[0] = (sS, sS.n)
        sc_free[1] = (sS, sS.n)
        avb_free = [(sV, sV.n), (sV, sV.n)]
    if "cck" in tap_dr:
        P.op("sync", lambda e: e.dma_start(
            out=tap_dr["cck"].ap(), in_=cc_outk.ap()),
            wait=[(sCC, 1)], inc=sOut, inc_amt=16)
    if "ccv" in tap_dr:
        P.op("sync", lambda e: e.dma_start(
            out=tap_dr["ccv"].ap(), in_=cc_outv.ap()),
            wait=[(sCC2, 1)], inc=sOut, inc_amt=16)
    attn_mm_tok = sT.n
    attn_tok = sV.n
    PAIRF[0] = (sS, sS.n)
    PAIRF[1] = (sV, attn_tok)
    PAIRF[2] = (sV, attn_tok)
    PAIRF[3] = (sV, attn_tok)
    tap("attn", attn_off, FC * NTOK)

    aob_off = R2
    first_aob = [True]
    sq_free2 = [(sT, sT.n), (sT, sT.n)]
    sum_guard = [(sV, attn_tok)]
    for oc in range(FC):
        k, wtok = w_load(lambda o=oc: dr["Wout_s"].ap()[o], FC)
        pi = oc % 2
        mm_block(k, wtok, FC,
                 lambda ic, th: A(attn_off + ic * NTOK + th * 512, 512), pi,
                 extra_wait=([(sV, attn_tok)] if oc == 0 else []))
        mmtok = sT.n
        for th in range(2):
            wt = [(sT, mmtok)]
            if first_aob[0]:
                wt.append((sT, attn_mm_tok))
                first_aob[0] = False
            P.v(lambda e, oc=oc, th=th, pi=pi: e.tensor_scalar(
                A(aob_off + oc * NTOK + th * 512, 512), PB[pi][th],
                c_bout[:, oc:oc + 1], None, ALU.add),
                wait=(wt if th == 0 else []),
                inc=(sV if th == 1 else None))
        evtok = sV.n
        PAIRF[pi] = (sV, evtok)
        P.s(lambda e, oc=oc: e.activation(
            A(SQREG + 0 * NTOK, NTOK), A(aob_off + oc * NTOK, NTOK),
            AF.Square),
            wait=[(sV, evtok), sq_free2[oc % 2]], inc=sS)
        sqtok = sS.n
        for th in range(2):
            P.t(lambda e, oc=oc, th=th: e.matmul(
                psD[0:1, th * 512:(th + 1) * 512], c_ocol[:, :],
                A(aob_off + oc * NTOK + th * 512, 512),
                start=(oc == 0), stop=(oc == FC - 1)),
                wait=([(sV, evtok)] + (sum_guard if oc == 0 else []))
                if th == 0 else [])
            P.t(lambda e, oc=oc, th=th: e.matmul(
                PB[2][th][0:1, :], c_ocol[:, :],
                A(SQREG + 0 * NTOK + th * 512, 512),
                start=(oc == 0), stop=(oc == FC - 1)),
                wait=([(sS, sqtok)] + ([PAIRF[2]] if oc == 0 else []))
                if th == 0 else [],
                inc=(sT if th == 1 else None))
        sq_free2[oc % 2] = (sT, sT.n)
    wout_mm_tok = sT.n
    rtok = ln_rows((sT, wout_mm_tok))
    btok = ln_bcast(rtok, 0, 1)
    x1b_off = R1
    first_x1b = [True]
    ot_free = [[(sV, sV.n)], [(sV, sV.n)]]
    sq_free3 = [(sT, sT.n), (sT, sT.n)]
    for ic in range(FC):
        sl = ic % 2
        P.op("sync", lambda e, sl=sl, ic=ic: e.dma_start(
            out=xt[sl][:, :], in_=dr["x_fm"].ap()[ic]),
            wait=xt_free[sl], inc=sxl[sl], inc_amt=16)
        ltok = sxl[sl].n
        for th in range(2):
            P.v(lambda e, ic=ic, th=th: e.tensor_tensor(
                t1s[th][:, :], A(aob_off + ic * NTOK + th * 512, 512), PB[0][th],
                ALU.mult),
                wait=([(sT, btok)] if (ic == 0 and th == 0) else []))
            P.v(lambda e, th=th: e.tensor_tensor(
                t1s[th][:, :], t1s[th][:, :], PB[1][th], ALU.subtract))
            wt = [(sxl[sl], ltok)] + (ot_free[sl] if th == 0 else [])
            P.v(lambda e, sl=sl, th=th: e.tensor_tensor(
                ot[sl][:, th * 512:(th + 1) * 512],
                xt[sl][:, th * 512:(th + 1) * 512],
                t1s[th][:, :], ALU.add),
                wait=wt, inc=(sV if th == 1 else None))
        x1tok = sV.n
        xt_free[sl] = [(sV, x1tok)]
        P.op("sync", lambda e, sl=sl, ic=ic: e.dma_start(
            out=x1_hbm.ap()[ic], in_=ot[sl][:, :]),
            wait=[(sV, x1tok)], inc=sXw, inc_amt=16)
        xwtok = sXw.n
        wt = [(sV, x1tok)]
        if first_x1b[0]:
            wt.append((sT, wout_mm_tok))
            first_x1b[0] = False
        P.v(lambda e, ic=ic, sl=sl: e.tensor_copy(
            A(x1b_off + ic * NTOK, NTOK), ot[sl][:, :]), wait=wt, inc=sV)
        cptok = sV.n
        P.s(lambda e, ic=ic, sl=sl: e.activation(
            A(SQREG + 0 * NTOK, NTOK), ot[sl][:, :], AF.Square),
            wait=[(sV, x1tok), sq_free3[ic % 2]], inc=sS)
        sqtok = sS.n
        ot_free[sl] = [(sXw, xwtok), (sV, cptok), (sS, sqtok)]
        for th in range(2):
            P.t(lambda e, ic=ic, th=th: e.matmul(
                psD[0:1, th * 512:(th + 1) * 512], c_ocol[:, :],
                A(x1b_off + ic * NTOK + th * 512, 512),
                start=(ic == 0), stop=(ic == FC - 1)),
                wait=([(sV, cptok)] + ([(sV, rtok)] if ic == 0 else []))
                if th == 0 else [])
            P.t(lambda e, ic=ic, th=th: e.matmul(
                PB[2][th][0:1, :], c_ocol[:, :],
                A(SQREG + 0 * NTOK + th * 512, 512),
                start=(ic == 0), stop=(ic == FC - 1)),
                wait=([(sS, sqtok)] + ([(sV, rtok)] if ic == 0 else []))
                if th == 0 else [],
                inc=(sT if th == 1 else None))
        sq_free3[ic % 2] = (sT, sT.n)
    ln2_stats_tok = sT.n
    sandwich_v_tok = sV.n
    tap("x1b", x1b_off, FC * NTOK)
    rtok = ln_rows((sT, ln2_stats_tok))
    btok = ln_bcast(rtok, 0, 1)
    h2_off = R0
    first_h2 = [True]
    for ic in range(FC):
        for th in range(2):
            wt = []
            if first_h2[0]:
                wt = [(sT, btok), (sT, qkv_mm_tok), (sV, sandwich_v_tok)]
                first_h2[0] = False
            P.v(lambda e, ic=ic, th=th: e.tensor_tensor(
                A(h2_off + ic * NTOK + th * 512, 512),
                A(x1b_off + ic * NTOK + th * 512, 512), PB[0][th], ALU.mult),
                wait=wt)
            P.v(lambda e, ic=ic, th=th: e.tensor_tensor(
                A(h2_off + ic * NTOK + th * 512, 512),
                A(h2_off + ic * NTOK + th * 512, 512), PB[1][th], ALU.subtract),
                inc=(sV if (ic == FC - 1 and th == 1) else None))
    h2_tok = sV.n
    PAIRF[0] = (sV, h2_tok)
    PAIRF[1] = (sV, h2_tok)
    tap("h2", h2_off, FC * NTOK)

    gfree = [PAIRF[3], PAIRF[3]]
    gate_defer = []
    cb_last = [0]
    for tc in range(8):
        gb = tc % 2
        for ic in range(FC):
            P.t(lambda e, tc=tc, ic=ic, gb=gb: e.matmul(
                psD[:, gb * 512:gb * 512 + 8],
                A(h2_off + ic * NTOK + tc * 128, 128),
                c_gate[:, ic * 8:(ic + 1) * 8],
                start=(ic == 0), stop=(ic == FC - 1)),
                wait=([gfree[gb], (sV, h2_tok), CSTG] if ic == 0 else []),
                inc=(sT if ic == FC - 1 else None))
        gmtok = sT.n
        L = psD[:, gb * 512:gb * 512 + 8]
        P.v(lambda e, L=L: e.reduce_max(pp[:, 0:1], L, AX.X), wait=[(sT, gmtok)])
        P.v(lambda e: e.drain())
        P.v(lambda e: e.tensor_scalar(pp[:, 1:2], pp[:, 0:1], -1.0, None, ALU.mult),
            inc=sV)
        vt = sV.n
        P.s(lambda e, L=L: e.activation(gsc[:, 0:8], L, AF.Exp, bias=pp[:, 1:2]),
            wait=[(sV, vt)], inc=sS)
        st = sS.n
        gfree[gb] = (sS, st)
        P.v(lambda e: e.reduce_sum(pp[:, 2:3], gsc[:, 0:8], AX.X), wait=[(sS, st)])
        P.v(lambda e: e.drain())
        P.v(lambda e: e.reciprocal(pp[:, 3:4], pp[:, 2:3]))
        P.v(lambda e: e.drain())
        P.v(lambda e, tc=tc: e.tensor_scalar(
            gpool[:, tc * 8:(tc + 1) * 8], gsc[:, 0:8], pp[:, 3:4], None, ALU.mult))
        P.v(lambda e: e.drain())
        P.v(lambda e, tc=tc: e.reduce_max(pp[:, 4:5], gpool[:, tc * 8:(tc + 1) * 8],
                                          AX.X))
        P.v(lambda e: e.drain())
        P.v(lambda e, tc=tc: e.tensor_scalar(
            gsc[:, 8:16], gpool[:, tc * 8:(tc + 1) * 8], pp[:, 4:5], None, ALU.is_ge))
        P.v(lambda e: e.drain())
        P.v(lambda e, tc=tc: e.scalar_tensor_tensor(
            gsc[:, 16:24], gsc[:, 8:16], -2.0, gpool[:, tc * 8:(tc + 1) * 8],
            ALU.mult, ALU.add))
        P.v(lambda e: e.drain())
        P.v(lambda e: e.reduce_max(pp[:, 5:6], gsc[:, 16:24], AX.X))
        P.v(lambda e: e.drain())
        P.v(lambda e, tc=tc: e.tensor_scalar(
            gsc[:, 8:16], gpool[:, tc * 8:(tc + 1) * 8], pp[:, 5:6], None, ALU.is_ge))
        P.v(lambda e: e.drain())
        P.v(lambda e, tc=tc: e.tensor_tensor(
            gpool[:, tc * 8:(tc + 1) * 8], gpool[:, tc * 8:(tc + 1) * 8],
            gsc[:, 8:16], ALU.mult))
        P.v(lambda e: e.drain())
        P.v(lambda e, tc=tc: e.tensor_copy(
            AR[:, PTREG + tc * 8:PTREG + (tc + 1) * 8],
            gpool[:, tc * 8:(tc + 1) * 8]),
            inc=sV)
        cbtok = sV.n
        def mk_tp(tc=tc, gb=gb, cbtok=cbtok):
            def em():
                P.t(lambda e, tc=tc, gb=gb: e.transpose(
                    psD.bitcast(BF16)[0:8, gb * 1024:gb * 1024 + 128],
                    AR[:, PTREG + tc * 8:PTREG + (tc + 1) * 8], c_ident[:, :]),
                    wait=[(sV, cb_last[0]), tfree[gb]], inc=sT)
                tttok = sT.n
                P.v(lambda e, tc=tc, gb=gb: e.tensor_copy(
                    AR[0:8, ROWB + tc * 128:ROWB + (tc + 1) * 128],
                    psD.bitcast(BF16)[0:8, gb * 1024:gb * 1024 + 128]),
                    wait=[(sT, tttok)], inc=sV)
                tfree[gb] = (sV, sV.n)
            return em
        gate_defer.append(mk_tp())
    gate_exp_tok = sS.n
    tfree = [(sS, gate_exp_tok), (sS, gate_exp_tok)]
    cb_last[0] = sV.n

    def hs_off(ic):
        return (R2 + ic * NTOK) if ic < 16 else (POOL + (ic - 16) * NTOK)

    y_off = R1
    mreg_free = (sT, attn_mm_tok)
    first_hs = [True]
    comb_tok = None
    for oc in range(ICS):
        if oc == 3:
            for f in gate_defer:
                f()
            comb_tok = sV.n
            PAIRF[3] = (sV, comb_tok)
            if "gpool" in tap_dr:
                P.op("sync", lambda e: e.dma_start(
                    out=tap_dr["gpool"].ap(), in_=gpool[:, :]),
                    wait=[(sV, comb_tok)], inc=sOut, inc_amt=16)
            if "comb" in tap_dr:
                P.op("sync", lambda e: e.dma_start(
                    out=tap_dr["comb"].ap()[0:8, 0:1024],
                    in_=AR[0:8, ROWB:ROWB + 1024]),
                    wait=[(sV, comb_tok)], inc=sOut, inc_amt=16)
        k1, wtok1 = w_load(lambda o=oc: dr["SW1_s"].ap()[o], FC)
        mm_block(k1, wtok1, FC,
                 lambda ic, th: A(h2_off + ic * NTOK + th * 512, 512), 0,
                 extra_wait=([(sV, h2_tok)] if oc == 0 else []))
        g1tok = sT.n
        k2, wtok2 = w_load(lambda o=oc: dr["SW2_s"].ap()[o], FC)
        mm_block(k2, wtok2, FC,
                 lambda ic, th: A(h2_off + ic * NTOK + th * 512, 512), 1)
        g2tok = sT.n
        for th in range(2):
            P.s(lambda e, th=th: e.activation(
                A(MREG + th * 512, 512), PB[0][th], AF.Silu),
                wait=([(sT, g1tok), mreg_free] if th == 0 else []),
                inc=(sS if th == 1 else None))
        silu_tok = sS.n
        PAIRF[0] = (sS, silu_tok)
        for th in range(2):
            wt = [(sT, g2tok), (sS, silu_tok)]
            if first_hs[0]:
                wt.append((sT, attn_mm_tok))
                first_hs[0] = False
            P.v(lambda e, oc=oc, th=th: e.tensor_tensor(
                A(hs_off(oc) + th * 512, 512), PB[1][th],
                A(MREG + th * 512, 512), ALU.mult),
                wait=(wt if th == 0 else []),
                inc=(sV if th == 1 else None))
        PAIRF[1] = (sV, sV.n)
        mreg_free = (sV, sV.n)
    hs_tok = sV.n
    first_y = [True]
    for oc in range(FC):
        k3, wtok3 = w_load(lambda o=oc: dr["SW3_s"].ap()[o], ICS)
        pe = 2 + oc % 2
        mm_block(k3, wtok3, ICS, lambda ic, th: A(hs_off(ic) + th * 512, 512), pe,
                 extra_wait=([(sV, hs_tok)] if oc == 0 else []))
        shtok = sT.n
        for th in range(2):
            wt = [(sT, shtok)]
            if first_y[0]:
                wt.append((sV, hs_tok))
                first_y[0] = False
            P.v(lambda e, oc=oc, th=th, pe=pe: e.tensor_copy(
                A(y_off + oc * NTOK + th * 512, 512), PB[pe][th]),
                wait=(wt if th == 0 else []),
                inc=(sV if th == 1 else None))
        PAIRF[pe] = (sV, sV.n)
    sw3_tok = sT.n

    def cb_off(e):
        return (R2 + 11264 + e * NTOK) if e < 5 else (POOL + 11264 + (e - 5) * NTOK)

    first_cb = [True]
    for e_ in range(8):
        for th in range(2):
            wt = [PAIRF[1]] if th == 0 else []
            if first_cb[0]:
                wt.append((sT, sw3_tok))
                wt.append((sV, comb_tok))
                first_cb[0] = False
            P.t(lambda ee, e_=e_, th=th: ee.matmul(
                PB[1][th], c_sel8[:, e_ * 128:(e_ + 1) * 128],
                AR[0:8, ROWB + th * 512:ROWB + (th + 1) * 512],
                start=True, stop=True),
                wait=wt, inc=(sT if th == 1 else None))
        bctok = sT.n
        for th in range(2):
            P.v(lambda ee, e_=e_, th=th: ee.tensor_copy(
                AR[:, cb_off(e_) + th * 512:cb_off(e_) + (th + 1) * 512],
                PB[1][th]),
                wait=([(sT, bctok)] if th == 0 else []),
                inc=(sV if th == 1 else None))
        PAIRF[1] = (sV, sV.n)
    cb_tok = sV.n

    he_free = [(sT, sw3_tok), (sT, sw3_tok)]
    for e_ in range(8):
        heb = POOL if e_ % 2 else R2
        for oc in range(ICM):
            k1, wtok1 = w_load(lambda e2=e_, o=oc: dr["W1_s"].ap()[e2, o], FC)
            mm_block(k1, wtok1, FC,
                     lambda ic, th: A(h2_off + ic * NTOK + th * 512, 512), 0)
            g1tok = sT.n
            k2, wtok2 = w_load(lambda e2=e_, o=oc: dr["W2_s"].ap()[e2, o], FC)
            mm_block(k2, wtok2, FC,
                     lambda ic, th: A(h2_off + ic * NTOK + th * 512, 512), 1)
            g2tok = sT.n
            for th in range(2):
                P.s(lambda e, th=th: e.activation(
                    A(MREG + th * 512, 512), PB[0][th], AF.Silu),
                    wait=([(sT, g1tok), mreg_free] if th == 0 else []),
                    inc=(sS if th == 1 else None))
            silu_tok = sS.n
            PAIRF[0] = (sS, silu_tok)
            for th in range(2):
                P.v(lambda e, e_=e_, th=th: e.tensor_tensor(
                    A(MREG + 1024 + th * 512, 512), PB[1][th],
                    AR[:, cb_off(e_) + th * 512:cb_off(e_) + (th + 1) * 512],
                    ALU.mult),
                    wait=([(sT, g2tok)] if th == 0 else []))
                wt = [(sS, silu_tok)]
                if oc == 0 and th == 0:
                    wt.append(he_free[e_ % 2])
                P.v(lambda e, oc=oc, th=th, heb=heb: e.tensor_tensor(
                    A(heb + oc * NTOK + th * 512, 512),
                    A(MREG + 1024 + th * 512, 512),
                    A(MREG + th * 512, 512), ALU.mult),
                    wait=wt, inc=(sV if th == 1 else None))
            PAIRF[1] = (sV, sV.n)
            mreg_free = (sV, sV.n)
        he_tok = sV.n
        for oc in range(FC):
            k3, wtok3 = w_load(lambda e2=e_, o=oc: dr["W3_s"].ap()[e2, o], ICM)
            pe = 2 + oc % 2
            mm_block(k3, wtok3, ICM,
                     lambda ic, th, heb=heb: A(heb + ic * NTOK + th * 512, 512), pe,
                     extra_wait=([(sV, he_tok)] if oc == 0 else []))
            eotok = sT.n
            for th in range(2):
                P.v(lambda e, oc=oc, th=th, pe=pe: e.tensor_tensor(
                    A(y_off + oc * NTOK + th * 512, 512), PB[pe][th],
                    A(y_off + oc * NTOK + th * 512, 512), ALU.add),
                    wait=([(sT, eotok)] if th == 0 else []),
                    inc=(sV if th == 1 else None))
            PAIRF[pe] = (sV, sV.n)
        he_free[e_ % 2] = (sT, sT.n)
    y_tok = sV.n
    ffn_mm_tok = sT.n
    tap("y", y_off, FC * NTOK)

    sq_free4 = [(sT, sT.n), (sT, sT.n)]
    for ic in range(FC):
        P.s(lambda e, ic=ic: e.activation(
            A(SQREG + 0 * NTOK, NTOK), A(y_off + ic * NTOK, NTOK), AF.Square),
            wait=[(sV, y_tok), sq_free4[ic % 2]], inc=sS)
        sqtok = sS.n
        for th in range(2):
            P.t(lambda e, ic=ic, th=th: e.matmul(
                psD[0:1, th * 512:(th + 1) * 512], c_ocol[:, :],
                A(y_off + ic * NTOK + th * 512, 512),
                start=(ic == 0), stop=(ic == FC - 1)),
                wait=([(sV, y_tok), PAIRF[3]] if (ic == 0 and th == 0) else []))
            P.t(lambda e, ic=ic, th=th: e.matmul(
                PB[2][th][0:1, :], c_ocol[:, :],
                A(SQREG + 0 * NTOK + th * 512, 512),
                start=(ic == 0), stop=(ic == FC - 1)),
                wait=([(sS, sqtok)] + ([PAIRF[2]] if ic == 0 else []))
                if th == 0 else [],
                inc=(sT if th == 1 else None))
        sq_free4[ic % 2] = (sT, sT.n)
    ln3_stats_tok = sT.n
    rtok = ln_rows((sT, ln3_stats_tok))
    btok = ln_bcast(rtok, 0, 1)
    for ic in range(FC):
        sl = ic % 2
        P.op("scalar", lambda e, sl=sl, ic=ic: e.dma_start(
            out=xt[sl][:, :], in_=x1_hbm.ap()[ic]),
            wait=xt_free[sl] + ([(sXw, sXw.n)] if ic == 0 else []),
            inc=sxl[sl], inc_amt=16)
        ltok = sxl[sl].n
        for th in range(2):
            P.v(lambda e, ic=ic, th=th: e.tensor_tensor(
                t1s[th][:, :], A(y_off + ic * NTOK + th * 512, 512), PB[0][th],
                ALU.mult),
                wait=([(sT, btok)] if (ic == 0 and th == 0) else []))
            P.v(lambda e, th=th: e.tensor_tensor(
                t1s[th][:, :], t1s[th][:, :], PB[1][th], ALU.subtract))
            wt = [(sxl[sl], ltok)] + (ot_free[sl] if th == 0 else [])
            P.v(lambda e, sl=sl, th=th: e.tensor_tensor(
                ot[sl][:, th * 512:(th + 1) * 512],
                xt[sl][:, th * 512:(th + 1) * 512], t1s[th][:, :], ALU.add),
                wait=wt, inc=(sV if th == 1 else None))
        otok = sV.n
        xt_free[sl] = [(sV, otok)]
        P.op("sync", lambda e, sl=sl, ic=ic: e.dma_start(
            out=out_fm.ap()[ic], in_=ot[sl][:, :]),
            wait=[(sV, otok)], inc=sOut, inc_amt=16)
        ot_free[sl] = [(sOut, sOut.n)]

    fin = sOut.n
    P.op("sync", lambda e: e.nop(), wait=[(sOut, fin)])

    global _LAST_Q
    _LAST_Q = {e: len(P.q[e]) for e in P.ENGINES}
    with nc.Block() as block:
        @block.tensor
        def _(e):
            for f in P.q["tensor"]:
                f(e)

        @block.vector
        def _(e):
            for f in P.q["vector"]:
                f(e)

        @block.scalar
        def _(e):
            for f in P.q["scalar"]:
                f(e)

        @block.gpsimd
        def _(e):
            for f in P.q["gpsimd"]:
                f(e)

        @block.sync
        def _(e):
            for f in P.q["sync"]:
                f(e)

    es.close()
    return nc


def kernel(**inputs):
    sh, pc = _prep_host(inputs)
    nc = build_nc()
    in_maps = [dict(sh, **pc[c]) for c in range(NCORE)]
    res = run_bass_kernel_spmd(nc, in_maps, core_ids=list(range(NCORE)))
    return _assemble_out(res.results)


# version 12 (final)
# speedup vs baseline: 1.0679x; 1.0021x over previous
import numpy as np
import ml_dtypes
import concourse.bass as bass
import concourse.mybir as mybir
from concourse.bass_utils import run_bass_kernel_spmd

BFNP = ml_dtypes.bfloat16
_LAST_Q = None
DEBUG_NO_PIPELINE = False
F32 = mybir.dt.float32
BF16 = mybir.dt.bfloat16
AF = mybir.ActivationFunctionType
ALU = mybir.AluOpType
AX = mybir.AxisListType

B, S, E, H, D = 2, 4096, 2048, 16, 128
NCORE = 8
NTOK = 1024
FC = 16
CH = 512
ICM = 11
ICS = 22
NEG = np.float32(-30000.0)
SCALE = float(np.float32(np.float32(2048.0) ** np.float32(-0.25)))
LN_EPS = float(np.float32(1e-5))
INV_E = float(np.float32(1.0) / np.float32(2048.0))
KHALF = 256


def _tile_wT(w):
    O, I = w.shape
    noc, nic = O // 128, I // 128
    t = w.reshape(noc, 128, nic, 128)
    t = np.transpose(t, (0, 3, 2, 1))
    return np.ascontiguousarray(t.reshape(noc, 128, nic * 128))


def _rope_consts():
    ar = np.arange(0, E, 2, dtype=np.float32)
    inv = np.power(np.float32(10000.0), -(ar / np.float32(E))).astype(np.float32)
    emb = np.repeat(inv, 2).astype(np.float32)
    arg = (np.float32(S) * emb).astype(np.float32)
    sin = np.sin(arg).astype(np.float32)
    cos = np.cos(arg).astype(np.float32)
    sinp = sin.copy()
    sinp[:E // 2] = -sinp[:E // 2]
    return sinp, cos


def _perm_mats():
    L = np.zeros((FC, 2, 128, 128), dtype=np.float32)
    for c in range(FC):
        s0 = 2 * c if c < 8 else 2 * (c - 8)
        for t in range(128):
            j = 128 * c + t
            p = 2 * j + 1 if j < E // 2 else 2 * (j - E // 2)
            s, r = p // 128, p % 128
            L[c, 0 if s == s0 else 1, r, t] = 1.0
    return L


UNITS = [(1, k) for k in range(16)] + [(0, k) for k in range(8)]
def _unit_masked(u):
    lq, kc2 = UNITS[u]
    return (lq == 1 and kc2 >= 8) or (lq == 0)
MASKED_IDX = {}
for _u in range(24):
    if _unit_masked(_u):
        MASKED_IDX[_u] = len(MASKED_IDX)
NMASK = len(MASKED_IDX)


def _masks_for(i):
    out = np.zeros((NMASK, 128, 1024), dtype=np.float32)
    q = np.arange(CH)
    kk = np.arange(128)
    for u, mi in MASKED_IDX.items():
        lq, kc2 = UNITS[u]
        g = (7 - i) if lq == 1 else i
        for ks in range(2):
            keyg = kc2 * KHALF + ks * 128 + kk[:, None]
            qg = g * CH + q[None, :]
            out[mi, :, ks * 512:(ks + 1) * 512] = np.where(keyg <= qg, 0.0, NEG)
    return out.astype(BFNP)


def _prep_host(inputs):
    f = lambda k: np.asarray(inputs[k], dtype=np.float32)
    Wqkv, bqkv = f("L0_attn_Wqkv"), f("L0_attn_bqkv")
    Wout, bout = f("L0_attn_Wout"), f("L0_attn_bout")
    gate = f("L0_ffn_gate_w")
    w1, w2, w3 = f("L0_ffn_w1"), f("L0_ffn_w2"), f("L0_ffn_w3")
    sw1, sw2, sw3 = f("L0_ffn_sw1"), f("L0_ffn_sw2"), f("L0_ffn_sw3")
    x = f("L0_attn_x")

    sh = {}
    qs = _tile_wT(Wqkv)
    order = list(range(16, 32)) + list(range(32, 48)) + list(range(0, 16))
    sh["Wqkv_s"] = np.ascontiguousarray(qs[order])
    sh["bqkv_sb"] = np.ascontiguousarray(bqkv.reshape(48, 128).T[:, order])
    sh["Wout_s"] = _tile_wT(Wout)
    sh["W1_s"] = np.ascontiguousarray(np.stack([_tile_wT(w1[e]) for e in range(8)]))
    sh["W2_s"] = np.ascontiguousarray(np.stack([_tile_wT(w2[e]) for e in range(8)]))
    sh["W3_s"] = np.ascontiguousarray(np.stack([_tile_wT(w3[e]) for e in range(8)]))
    sh["SW1_s"] = _tile_wT(sw1)
    sh["SW2_s"] = _tile_wT(sw2)
    sh["SW3_s"] = _tile_wT(sw3)
    sh["bout_sb"] = np.ascontiguousarray(bout.reshape(16, 128).T)
    g = gate.reshape(8, FC, 128)
    sh["gate_sb"] = np.ascontiguousarray(np.transpose(g, (2, 1, 0)).reshape(128, FC * 8))
    sinp, cos = _rope_consts()
    sh["cos_sb"] = np.ascontiguousarray(cos.reshape(FC, 128).T)
    sh["sinp_sb"] = np.ascontiguousarray(sinp.reshape(FC, 128).T)
    sh["L_sb"] = np.ascontiguousarray(
        np.transpose(_perm_mats(), (2, 0, 1, 3)).reshape(128, FC * 2 * 128))
    sh["ident"] = np.eye(128, dtype=np.float32)
    Emat = np.zeros((128, 256), dtype=np.float32)
    for hh in range(16):
        Emat[:, hh * 16 + hh] = 1.0
    sh["Emat"] = Emat
    sel16 = np.zeros((16, 2048), dtype=np.float32)
    for hh in range(16):
        sel16[hh, hh * 128:(hh + 1) * 128] = 1.0
    sh["sel16"] = sel16
    sel4 = np.zeros((4, 512), dtype=np.float32)
    for k in range(4):
        sel4[k, k * 128:(k + 1) * 128] = 1.0
    sh["sel4"] = sel4
    sel8 = np.zeros((8, 1024), dtype=np.float32)
    for k in range(8):
        sel8[k, k * 128:(k + 1) * 128] = 1.0
    sh["sel8"] = sel8
    sh["ones_col"] = np.ones((128, 1), dtype=np.float32)
    sh["ones_row"] = np.ones((1, 128), dtype=np.float32)
    sh["epsv"] = np.broadcast_to(
        np.array([LN_EPS, 1e-12, 0.0, 0.0], dtype=np.float32), (128, 4)).copy()

    pc = []
    for c in range(NCORE):
        b, i = c // 4, c % 4
        rows = np.concatenate(
            [x[b, i * CH:(i + 1) * CH], x[b, (7 - i) * CH:(8 - i) * CH]], axis=0)
        xfm = np.ascontiguousarray(rows.T).reshape(FC, 128, NTOK)
        pc.append({"x_fm": xfm, "masks": _masks_for(i)})
    return sh, pc


def _assemble_out(results):
    out = np.empty((B, S, E), dtype=np.float32)
    for c in range(NCORE):
        b, i = c // 4, c % 4
        ofm = np.asarray(results[c]["out_fm"]).reshape(E, NTOK)
        rows = ofm.T
        out[b, i * CH:(i + 1) * CH] = rows[:CH]
        out[b, (7 - i) * CH:(8 - i) * CH] = rows[CH:]
    return np.ascontiguousarray(out)



class Sem:
    __slots__ = ("h", "n")
    def __init__(self, h):
        self.h = h
        self.n = 0


class Prog:
    ENGINES = ("tensor", "vector", "scalar", "gpsimd", "sync")

    def __init__(self):
        self.q = {e: [] for e in self.ENGINES}

    def op(self, eng, fn, wait=(), inc=None, inc_amt=1):
        w = [(s, v) for (s, v) in wait if s is not None and v > 0]
        if inc is not None:
            inc.n += inc_amt
        def run(engine, _w=w, _fn=fn, _inc=inc, _amt=inc_amt):
            for s, v in _w:
                engine.wait_ge(s.h, v)
            inst = _fn(engine)
            if _inc is not None:
                inst.then_inc(_inc.h, _amt)
        self.q[eng].append(run)
        return inc.n if inc is not None else 0

    def t(self, fn, wait=(), inc=None):
        return self.op("tensor", fn, wait, inc, 1)

    def v(self, fn, wait=(), inc=None):
        return self.op("vector", fn, wait, inc, 1)

    def s(self, fn, wait=(), inc=None):
        return self.op("scalar", fn, wait, inc, 1)



def build_nc(taps=(), attn_units=None):
    from contextlib import ExitStack
    nc = bass.Bass("TRN2", target_bir_lowering=False)
    taps = set(taps)
    P = Prog()
    es = ExitStack()
    sb = lambda name, shape, dt: es.enter_context(nc.sbuf_tensor(name, list(shape), dt))

    dr = {}
    def din(name, shape, dt=F32):
        dr[name] = nc.dram_tensor(name, list(shape), dt, kind="ExternalInput")
    din("x_fm", (FC, 128, NTOK))
    din("masks", (NMASK, 128, 1024), BF16)
    din("Wqkv_s", (48, 128, 2048)); din("Wout_s", (16, 128, 2048))
    din("W1_s", (8, ICM, 128, 2048)); din("W2_s", (8, ICM, 128, 2048))
    din("W3_s", (8, FC, 128, ICM * 128))
    din("SW1_s", (ICS, 128, 2048)); din("SW2_s", (ICS, 128, 2048))
    din("SW3_s", (FC, 128, ICS * 128))
    din("bqkv_sb", (128, 48)); din("bout_sb", (128, 16))
    din("gate_sb", (128, FC * 8))
    din("cos_sb", (128, 16)); din("sinp_sb", (128, 16))
    din("L_sb", (128, FC * 2 * 128))
    din("ident", (128, 128)); din("Emat", (128, 256))
    din("sel16", (16, 2048))
    din("sel4", (4, 512)); din("sel8", (8, 1024))
    din("ones_col", (128, 1)); din("ones_row", (1, 128))
    din("epsv", (128, 4))

    out_fm = nc.dram_tensor("out_fm", [FC, 128, NTOK], F32, kind="ExternalOutput")
    cc_ink = nc.dram_tensor("cc_ink", [E * NTOK], BF16, kind="Internal")
    cc_inv = nc.dram_tensor("cc_inv", [E * NTOK], BF16, kind="Internal")
    cc_outk = nc.dram_tensor("cc_outk", [4, E * NTOK], BF16, kind="Internal")
    cc_outv = nc.dram_tensor("cc_outv", [4, E * NTOK], BF16, kind="Internal")
    x1_hbm = nc.dram_tensor("x1_hbm", [FC, 128, NTOK], F32, kind="Internal")
    tap_dr = {}
    for tn in taps:
        if tn == "gpool":
            tap_dr[tn] = nc.dram_tensor("tap_gpool", [128, 64], F32,
                                        kind="ExternalOutput")
        elif tn == "ccinv":
            tap_dr[tn] = nc.dram_tensor("tap_ccinv", [E * NTOK], BF16,
                                        kind="ExternalOutput")
        elif tn in ("cck", "ccv"):
            tap_dr[tn] = nc.dram_tensor("tap_" + tn, [4, E * NTOK], BF16,
                                        kind="ExternalOutput")
        elif tn == "rsum":
            tap_dr[tn] = nc.dram_tensor("tap_rsum", [4, 512], F32,
                                        kind="ExternalOutput")
        else:
            tap_dr[tn] = nc.dram_tensor("tap_" + tn, [128, FC * NTOK], BF16,
                                        kind="ExternalOutput")

    AR_N = 70656
    AR = sb("AR", (128, AR_N), BF16)
    R0, R1, R2, POOL = 0, 16384, 32768, 49152
    MREG, PTREG, SQREG, ROWB = 65536, 66560, 68608, 69632
    def A(off, n):
        return AR[:, off:off + n]

    wbuf = [sb("wb0", (128, ICS * 128), BF16), sb("wb1", (128, ICS * 128), BF16),
            sb("wb2", (128, 2048), BF16), sb("wb3", (128, 2048), BF16),
            sb("wb4", (128, 2048), BF16), sb("wb5", (128, 2048), BF16)]
    xt = [sb(f"xt{k}", (128, NTOK), F32) for k in range(2)]
    ot = [sb(f"ot{k}", (128, NTOK), F32) for k in range(2)]
    t1s = [sb(f"t1s{k}", (128, 512), F32) for k in range(2)]
    rows = sb("rows", (1, 2048), F32)
    rs16 = sb("rs16", (16, 512), F32)
    rbsb = sb("rbsb", (128, 1024), F32)
    pp = sb("pp", (128, 8), F32)
    gsc = sb("gsc", (128, 24), F32)
    gpool = sb("gpool", (128, 64), F32)

    c_bqkv = sb("c_bqkv", (128, 48), F32)
    c_bout = sb("c_bout", (128, 16), F32)
    c_cos = sb("c_cos", (128, 16), F32)
    c_sinp = sb("c_sinp", (128, 16), F32)
    c_eps = sb("c_eps", (128, 4), F32)
    c_orow = sb("c_orow", (1, 128), F32)
    c_gate = sb("c_gate", (128, FC * 8), BF16)
    c_ident = sb("c_ident", (128, 128), BF16)
    c_E = sb("c_E", (128, 256), BF16)
    c_sel16 = sb("c_sel16", (16, 2048), BF16)
    c_ocol = sb("c_ocol", (128, 1), BF16)
    c_sel8 = sb("c_sel8", (8, 1024), BF16)

    psALL = es.enter_context(nc.psum_tensor("psALL", (128, 4096), F32))
    psA = psALL[:, 0:1024]
    psB = psALL[:, 1024:2048]
    psC = psALL[:, 2048:3072]
    psD = psALL[:, 3072:4096]
    PB = [
        (psA[:, 0:512], psA[:, 512:1024]),
        (psB[:, 0:512], psB[:, 512:1024]),
        (psC[:, 0:512], psC[:, 512:1024]),
        (psD[:, 0:512], psD[:, 512:1024]),
    ]

    mksem = lambda name: Sem(es.enter_context(nc.semaphore(name)))
    sT, sV, sS = mksem("sT"), mksem("sV"), mksem("sS")
    swl = [mksem(f"swl{k}") for k in range(6)]
    sxl = [mksem(f"sxl{k}") for k in range(2)]
    sml = [mksem(f"sml{k}") for k in range(2)]
    skl = [mksem(f"skl{k}") for k in range(2)]
    svl = [mksem(f"svl{k}") for k in range(2)]
    sStg = mksem("sStg")
    sCC = mksem("sCC")
    sCC2 = mksem("sCC2")
    sXw = mksem("sXw")
    sOut = mksem("sOut")
    sCg = mksem("sCg")
    sCs = mksem("sCs")

    PAIRF = [(None, 0)] * 4

    wfree = [(None, 0)] * 6
    wrr = [0]
    wrr_big = [0]

    def w_load(ap_fn, nic):
        if nic > 16:
            k = wrr_big[0] % 2
            wrr_big[0] += 1
        else:
            k = 2 + wrr[0] % 4
            wrr[0] += 1
        P.op("gpsimd",
             lambda e, k=k, n=nic, f=ap_fn: e.dma_start(out=wbuf[k][:, :n * 128], in_=f()),
             wait=[wfree[k]], inc=swl[k], inc_amt=16)
        return k, swl[k].n

    def mm_block(k, wtok, nic, rhs_fn, pi, extra_wait=()):
        for ic in range(nic):
            for th in range(2):
                wt = ([(swl[k], wtok), PAIRF[pi]] + list(extra_wait)
                      if (ic == 0 and th == 0) else [])
                last = (ic == nic - 1 and th == 1)
                P.t(lambda e, k=k, ic=ic, th=th, pi=pi, f=(ic == 0), l=(ic == nic - 1):
                    e.matmul(PB[pi][th], wbuf[k][:, ic * 128:(ic + 1) * 128],
                             rhs_fn(ic, th), start=f, stop=l),
                    wait=wt, inc=(sT if last else None))
        wfree[k] = (sT, sT.n)
        return sT.n

    MEAN = rows[0:1, 0:1024]
    TMP = rows[0:1, 1024:2048]
    T2 = rbsb[0:1, 0:1024]

    def ln_rows(wait_tok):
        P.v(lambda e: e.tensor_scalar(MEAN, psD[0:1, 0:1024], INV_E, None, ALU.mult),
            wait=[wait_tok])
        P.v(lambda e: e.tensor_scalar(rows[0:1, 1024:1536], PB[2][0][0:1, :], INV_E,
                                      None, ALU.mult))
        P.v(lambda e: e.tensor_scalar(rows[0:1, 1536:2048], PB[2][1][0:1, :], INV_E,
                                      None, ALU.mult))
        P.v(lambda e: e.drain())
        P.v(lambda e: e.tensor_tensor(T2, MEAN, MEAN, ALU.mult))
        P.v(lambda e: e.drain())
        P.v(lambda e: e.tensor_tensor(TMP, TMP, T2, ALU.subtract), inc=sV)
        vt = sV.n
        P.s(lambda e: e.activation(T2, TMP, AF.Sqrt, bias=c_eps[0:1, 0:1]),
            wait=[(sV, vt)], inc=sS)
        st = sS.n
        P.v(lambda e: e.reciprocal(TMP, T2), wait=[(sS, st)])
        P.v(lambda e: e.drain())
        P.v(lambda e: e.tensor_tensor(MEAN, MEAN, TMP, ALU.mult), inc=sV)
        PAIRF[2] = (sV, sV.n)
        PAIRF[3] = (sV, sV.n)
        return sV.n

    def ln_bcast(rows_tok, pir, pim):
        for th in range(2):
            P.t(lambda e, th=th, pir=pir: e.matmul(
                PB[pir][th], c_orow[0:1, :], TMP[0:1, th * 512:(th + 1) * 512],
                start=True, stop=True),
                wait=([(sV, rows_tok), PAIRF[pir], PAIRF[pim]] if th == 0 else []))
            P.t(lambda e, th=th, pim=pim: e.matmul(
                PB[pim][th], c_orow[0:1, :], MEAN[0:1, th * 512:(th + 1) * 512],
                start=True, stop=True),
                inc=(sT if th == 1 else None))
        return sT.n

    def tap(name, off, n):
        if name in tap_dr:
            P.op("sync", lambda e, name=name, off=off, n=n: e.dma_start(
                out=tap_dr[name].ap()[:, 0:n], in_=A(off, n)),
                wait=[(sV, sV.n), (sS, sS.n), (sT, sT.n)], inc=sOut, inc_amt=16)

    for name, buf in [("bqkv_sb", c_bqkv), ("bout_sb", c_bout), ("cos_sb", c_cos),
                      ("sinp_sb", c_sinp), ("epsv", c_eps), ("ones_row", c_orow)]:
        P.op("sync", lambda e, b=buf, n=name: e.dma_start(out=b[:, :], in_=dr[n].ap()),
             inc=sCs, inc_amt=16)
    for name, buf in [("gate_sb", c_gate), ("ident", c_ident), ("Emat", c_E),
                      ("ones_col", c_ocol), ("sel8", c_sel8), ("sel16", c_sel16)]:
        P.op("gpsimd", lambda e, b=buf, n=name: e.dma_start(out=b[:, :], in_=dr[n].ap()),
             inc=sCg, inc_amt=16)
    P.op("gpsimd", lambda e: e.dma_start(out=A(POOL, FC * 2 * 128), in_=dr["L_sb"].ap()),
         inc=sCg, inc_amt=16)
    CSTS = (sCs, sCs.n)
    CSTG = (sCg, sCg.n)

    xb_off, h_off = R1, R0
    xt_free = [[CSTS], [CSTS]]
    sq_free = [(None, 0), (None, 0)]
    for ic in range(FC):
        sl = ic % 2
        P.op("sync", lambda e, sl=sl, ic=ic: e.dma_start(
            out=xt[sl][:, :], in_=dr["x_fm"].ap()[ic]),
            wait=xt_free[sl], inc=sxl[sl], inc_amt=16)
        ltok = sxl[sl].n
        P.v(lambda e, ic=ic, sl=sl: e.tensor_copy(
            A(xb_off + ic * NTOK, NTOK), xt[sl][:, :]),
            wait=[(sxl[sl], ltok)], inc=sV)
        vtok = sV.n
        P.s(lambda e, sl=sl, ic=ic: e.activation(
            A(SQREG + 0 * NTOK, NTOK), xt[sl][:, :], AF.Square),
            wait=[(sxl[sl], ltok), sq_free[ic % 2]], inc=sS)
        stok = sS.n
        xt_free[sl] = [(sV, vtok), (sS, stok)]
        for th in range(2):
            P.t(lambda e, ic=ic, th=th: e.matmul(
                psD[0:1, th * 512:(th + 1) * 512], c_ocol[:, :],
                A(xb_off + ic * NTOK + th * 512, 512),
                start=(ic == 0), stop=(ic == FC - 1)),
                wait=([(sV, vtok), CSTG] if th == 0 else []))
            P.t(lambda e, ic=ic, th=th: e.matmul(
                PB[2][th][0:1, :], c_ocol[:, :],
                A(SQREG + 0 * NTOK + th * 512, 512),
                start=(ic == 0), stop=(ic == FC - 1)),
                wait=([(sS, stok)] if th == 0 else []),
                inc=(sT if th == 1 else None))
        sq_free[ic % 2] = (sT, sT.n)
    stats_tok = sT.n
    rtok = ln_rows((sT, stats_tok))
    btok = ln_bcast(rtok, 1, 0)
    for ic in range(FC):
        for th in range(2):
            P.v(lambda e, ic=ic, th=th: e.tensor_tensor(
                A(h_off + ic * NTOK + th * 512, 512),
                A(xb_off + ic * NTOK + th * 512, 512), PB[1][th], ALU.mult),
                wait=([(sT, btok)] if (ic == 0 and th == 0) else []))
            P.v(lambda e, ic=ic, th=th: e.tensor_tensor(
                A(h_off + ic * NTOK + th * 512, 512),
                A(h_off + ic * NTOK + th * 512, 512), PB[0][th], ALU.subtract),
                inc=(sV if (ic == FC - 1 and th == 1) else None))
    h_tok = sV.n
    PAIRF[0] = (sV, h_tok)
    PAIRF[1] = (sV, h_tok)
    PAIRF[2] = (sV, rtok)
    PAIRF[3] = (sV, rtok)
    tap("h", h_off, FC * NTOK)

    def qkv_blocks(idx0, evac_fn):
        for n in range(16):
            k, wtok = w_load(lambda o=idx0 + n: dr["Wqkv_s"].ap()[o], FC)
            pi = n % 2
            mm_block(k, wtok, FC,
                     lambda ic, th: A(h_off + ic * NTOK + th * 512, 512), pi)
            evac_fn(n, pi, sT.n)
            PAIRF[pi] = (sV, sV.n)

    def evac_qk(dst_off, bias0):
        def ev(oc, pi, mmtok):
            for th in range(2):
                P.v(lambda e, oc=oc, th=th, pi=pi: e.tensor_scalar(
                    A(dst_off + oc * NTOK + th * 512, 512), PB[pi][th],
                    c_bqkv[:, bias0 + oc:bias0 + oc + 1], None, ALU.add),
                    wait=([(sT, mmtok)] if th == 0 else []),
                    inc=(sV if th == 1 else None))
        return ev

    def rope_norm(raw_off, rot_off, sq_init, extra_rot_wait=()):
        stok_sq = [sq_init, sq_init]
        first_rot = [True]
        for c in range(FC):
            s0 = 2 * c if c < 8 else 2 * (c - 8)
            pi = c % 2
            for th in range(2):
                for j in range(2):
                    wt = [PAIRF[pi]] if (th == 0 and j == 0) else []
                    P.t(lambda e, c=c, th=th, j=j, s=(s0 + j), pi=pi: e.matmul(
                        PB[pi][th], AR[:, POOL + (c * 2 + j) * 128:
                                       POOL + (c * 2 + j + 1) * 128],
                        A(raw_off + s * NTOK + th * 512, 512),
                        start=(j == 0), stop=(j == 1)),
                        wait=wt, inc=(sT if (th == 1 and j == 1) else None))
            xrtok = sT.n
            for th in range(2):
                P.v(lambda e, c=c, th=th: e.tensor_scalar(
                    t1s[th][:, :], A(raw_off + c * NTOK + th * 512, 512),
                    c_cos[:, c:c + 1], None, ALU.mult),
                    wait=([(sT, xrtok)] +
                          (list(extra_rot_wait) if first_rot[0] else []))
                    if th == 0 else [])
                first_rot[0] = False
                P.v(lambda e, c=c, th=th, pi=pi: e.scalar_tensor_tensor(
                    A(rot_off + c * NTOK + th * 512, 512), PB[pi][th],
                    c_sinp[:, c:c + 1], t1s[th][:, :], ALU.mult, ALU.add),
                    inc=(sV if th == 1 else None))
            PAIRF[pi] = (sV, sV.n)
            rottok = sV.n
            P.s(lambda e, c=c: e.activation(
                A(SQREG + 0 * NTOK, NTOK), A(rot_off + c * NTOK, NTOK),
                AF.Square),
                wait=[(sV, rottok), stok_sq[c % 2]], inc=sS)
            sqtok = sS.n
            for th in range(2):
                P.t(lambda e, c=c, th=th: e.matmul(
                    psD[0:1, th * 512:(th + 1) * 512], c_ocol[:, :],
                    A(SQREG + 0 * NTOK + th * 512, 512),
                    start=(c == 0), stop=(c == FC - 1)),
                    wait=([(sS, sqtok)] + ([PAIRF[3]] if c == 0 else []))
                    if th == 0 else [],
                    inc=(sT if th == 1 else None))
            stok_sq[c % 2] = (sT, sT.n)
        n2tok = sT.n
        P.s(lambda e: e.activation(T2, psD[0:1, 0:1024], AF.Sqrt),
            wait=[(sT, n2tok)], inc=sS)
        P.v(lambda e: e.tensor_scalar(T2, T2, 1e-12, None, ALU.max),
            wait=[(sS, sS.n)])
        P.v(lambda e: e.drain())
        P.v(lambda e: e.reciprocal(TMP, T2))
        P.v(lambda e: e.drain())
        P.v(lambda e: e.tensor_scalar(TMP, TMP, SCALE, None, ALU.mult), inc=sV)
        rntok = sV.n
        PAIRF[3] = (sS, sS.n)
        for th in range(2):
            P.t(lambda e, th=th: e.matmul(
                PB[2][th], c_orow[0:1, :], TMP[0:1, th * 512:(th + 1) * 512],
                start=True, stop=True),
                wait=([(sV, rntok), PAIRF[2]] if th == 0 else []),
                inc=(sT if th == 1 else None))
        bctok = sT.n
        for c in range(FC):
            for th in range(2):
                P.v(lambda e, c=c, th=th: e.tensor_tensor(
                    A(rot_off + c * NTOK + th * 512, 512),
                    A(rot_off + c * NTOK + th * 512, 512), PB[2][th], ALU.mult),
                    wait=([(sT, bctok)] if (c == 0 and th == 0) else []),
                    inc=(sV if (c == FC - 1 and th == 1) else None))
        PAIRF[2] = (sV, sV.n)
        return sV.n

    kraw_off, krot_off = R2, R1
    vtm_off = R2
    vev_off = POOL + 4096
    qkv_blocks(0, evac_qk(kraw_off, 0))
    krot_tok = rope_norm(kraw_off, krot_off, (sT, stats_tok), extra_rot_wait=[CSTG])
    k_stat_done = (sT, sT.n)
    tap("krot", krot_off, FC * NTOK)

    cck = cc_ink.ap().rearrange("(fc p t) -> fc p t", fc=FC, p=128)
    for g4 in range(4):
        P.op("sync", lambda e, g=g4: e.dma_start(
            out=cck[g * 4:(g + 1) * 4].rearrange("fc p t -> p fc t"),
            in_=A(krot_off + g * 4 * NTOK, 4 * NTOK).rearrange(
                "p (fc t) -> p fc t", t=NTOK)),
            wait=[(sV, krot_tok)], inc=sStg, inc_amt=16)
    kstg_tok = sStg.n
    P.op("gpsimd", lambda e: e.collective_compute(
        "AllGather", ALU.bypass,
        replica_groups=[[0, 1, 2, 3], [4, 5, 6, 7]],
        ins=[cc_ink.ap()], outs=[cc_outk.ap()]),
        wait=[(sStg, kstg_tok)], inc=sCC, inc_amt=1)

    vev_free = [(sS, sS.n)]
    tp_free = [(sV, krot_tok), (sV, krot_tok)]

    def v_block(n, pi):
        k2, wtok = w_load(lambda o=16 + n: dr["Wqkv_s"].ap()[o], FC)
        mm_block(k2, wtok, FC,
                 lambda ic, th: A(h_off + ic * NTOK + th * 512, 512), pi)
        mmtok = sT.n
        for th in range(2):
            P.s(lambda e, n=n, th=th, pi=pi: e.activation(
                A(vev_off + th * 512, 512), PB[pi][th], AF.Identity,
                bias=c_bqkv[:, 16 + n:17 + n]),
                wait=([(sT, mmtok), vev_free[0]] if th == 0 else []),
                inc=(sS if th == 1 else None))
        evtok = sS.n
        PAIRF[pi] = (sS, evtok)
        last_tt = 0
        for tc in range(8):
            pt = tc % 2
            P.t(lambda e, tc=tc, pt=pt: e.transpose(
                psC.bitcast(BF16)[:, pt * 1024:pt * 1024 + 128],
                A(vev_off + tc * 128, 128), c_ident[:, :]),
                wait=([(sS, evtok)] if tc < 2 else []) + [tp_free[pt]],
                inc=sT)
            tttok = sT.n
            last_tt = tttok
            P.s(lambda e, n=n, tc=tc, pt=pt: e.copy(
                A(vtm_off + tc * E + n * 128, 128),
                psC.bitcast(BF16)[:, pt * 1024:pt * 1024 + 128]),
                wait=[(sT, tttok)], inc=sS)
            tp_free[pt] = (sS, sS.n)
        vev_free[0] = (sT, last_tt)

    for n in range(FC):
        v_block(n, n % 2)
    PAIRF[2] = (sS, sS.n)
    vtm_s_tok = sS.n
    ccv = cc_inv.ap().rearrange("(tc p f) -> tc p f", tc=8, p=128)
    for g2 in range(2):
        P.op("sync", lambda e, g=g2: e.dma_start(
            out=ccv[g * 4:(g + 1) * 4].rearrange("tc p f -> p tc f"),
            in_=A(vtm_off + g * 4 * E, 4 * E).rearrange(
                "p (tc f) -> p tc f", f=E)),
            wait=[(sS, vtm_s_tok)], inc=sStg, inc_amt=16)
    vstg_tok = sStg.n
    if "ccinv" in tap_dr:
        P.op("sync", lambda e: e.dma_start(
            out=tap_dr["ccinv"].ap(), in_=cc_inv.ap()),
            wait=[(sStg, vstg_tok)], inc=sOut, inc_amt=16)
    P.op("gpsimd", lambda e: e.collective_compute(
        "AllGather", ALU.bypass,
        replica_groups=[[0, 1, 2, 3], [4, 5, 6, 7]],
        ins=[cc_inv.ap()], outs=[cc_outv.ap()]),
        wait=[(sStg, vstg_tok)], inc=sCC2, inc_amt=1)

    qraw_off, qrot_off = R1, R2
    first_q = [True]

    def evac_q(oc, pi, mmtok):
        for th in range(2):
            wt = [(sT, mmtok)]
            if first_q[0]:
                wt.append((sStg, kstg_tok))
                first_q[0] = False
            P.v(lambda e, oc=oc, th=th, pi=pi: e.tensor_scalar(
                A(qraw_off + oc * NTOK + th * 512, 512), PB[pi][th],
                c_bqkv[:, 32 + oc:32 + oc + 1], None, ALU.add),
                wait=(wt if th == 0 else []),
                inc=(sV if th == 1 else None))

    qkv_blocks(32, evac_q)
    qkv_mm_tok = sT.n
    qrot_tok = rope_norm(qraw_off, qrot_off, k_stat_done,
                         extra_rot_wait=[(sStg, vstg_tok)])
    tap("qrot", qrot_off, FC * NTOK)

    attn_off = R1
    KS = [POOL, POOL + 4096]
    VS = [POOL + 8192, POOL + 12288]
    MS = [MREG, MREG]
    SPA = [psALL[:, 0:1024], psALL[:, 2560:3584]]
    AVB2 = [psALL[:, 1024:1536], psALL[:, 1536:2048]]
    RSB16 = psALL[0:16, 3584:4096]
    AVT_SB = A(R0, 16384).bitcast(F32)

    ck = cc_outk.ap().rearrange("r (fc p t) -> r fc p t", fc=FC, p=128)

    kslot_free = [(sStg, vstg_tok), (sStg, vstg_tok)]
    vslot_free = [(sStg, vstg_tok), (sStg, vstg_tok)]
    pool_guard = [(sV, qrot_tok)]
    mslot_free = [CSTG, CSTG]
    pT_free = [(sV, qrot_tok), (sV, qrot_tok)]
    sc_free = [(sV, qrot_tok), (sV, qrot_tok)]
    avb_free = [(sV, qrot_tok), (sV, qrot_tok)]
    avt_guard = [(sT, qkv_mm_tok)]
    rs_free = [(sS, sS.n)]
    rb_free = [(sS, sS.n), (sS, sS.n)]
    uh = [0]
    kld, vld, mld = [0], [0], [0]
    attn_guard = [(sV, sV.n), (sS, sS.n)]
    pend = [None]

    def flushp():
        if pend[0] is not None:
            pend[0]()
            pend[0] = None

    for lq in (1, 0):
        nk0 = 16 if lq == 1 else 8
        kc2_list = list(range(nk0)) if attn_units is None else attn_units.get(lq, [])
        if not kc2_list:
            continue
        nk = len(kc2_list)
        for kpos, kc2 in enumerate(kc2_list):
            u = kc2 if lq == 1 else 16 + kc2
            masked = u in MASKED_IDX
            j, hhf = kc2 // 2, kc2 % 2
            r = min(j, 7 - j)
            off = (0 if j < 4 else 512) + hhf * KHALF
            tcbase = (0 if j < 4 else 4) + hhf * 2
            ksl = kld[0] % 2
            P.op("scalar", lambda e, r=r, off=off, ksl=ksl: e.dma_start(
                out=AR[:, KS[ksl]:KS[ksl] + FC * KHALF].rearrange(
                    "p (fc t) -> p fc t", t=KHALF),
                in_=ck[r, :, :, off:off + KHALF].rearrange(
                    "fc p t -> p fc t")),
                wait=[kslot_free[ksl], (sCC, 1)] +
                     (pool_guard if kld[0] < 2 else []),
                inc=skl[ksl], inc_amt=16)
            ktok = skl[ksl].n
            kld[0] += 1
            vsl = vld[0] % 2
            P.op("sync", lambda e, r=r, tb=tcbase, vsl=vsl: e.dma_start(
                out=AR[:, VS[vsl]:VS[vsl] + 2 * E].rearrange(
                    "p (tc f) -> p tc f", f=E),
                in_=cc_outv.ap()[r:r + 1,
                                 tb * 128 * E:(tb + 2) * 128 * E].rearrange(
                    "o (tc p f) -> (o p) tc f", tc=2, p=128)),
                wait=[vslot_free[vsl], (sCC2, 1)] +
                     (pool_guard if vld[0] < 2 else []),
                inc=svl[vsl], inc_amt=16)
            vtok = svl[vsl].n
            vld[0] += 1
            mtok = 0
            if masked:
                msl = 0
                mi = MASKED_IDX[u]
                P.op("sync", lambda e, mi=mi, msl=msl: e.dma_start(
                    out=A(MS[msl], 1024), in_=dr["masks"].ap()[mi]),
                    wait=[mslot_free[msl]], inc=sml[msl], inc_amt=16)
                mtok = sml[msl].n
                mld[0] += 1
            first_kc = (kpos == 0)
            for h in range(16):
                i_uh = uh[0]
                uh[0] += 1
                sp = i_uh % 2
                ptb = PTREG + sp * 1024
                wt = [sc_free[sp]]
                if h == 0:
                    wt.append((skl[ksl], ktok))
                if i_uh == 0:
                    wt += attn_guard + [(sV, qrot_tok)]
                for ksub in range(2):
                    P.t(lambda e, h=h, ksub=ksub, ksl=ksl, lq=lq, sp=sp,
                        st=(not masked): e.matmul(
                        SPA[sp][:, ksub * 512:(ksub + 1) * 512],
                        AR[:, KS[ksl] + h * KHALF + ksub * 128:
                           KS[ksl] + h * KHALF + ksub * 128 + 128],
                        A(qrot_off + h * NTOK + lq * 512, 512),
                        start=True, stop=st),
                        wait=(wt if ksub == 0 else []),
                        inc=(sT if (ksub == 1 and not masked) else None))
                if masked:
                    for ksub in range(2):
                        P.t(lambda e, ksub=ksub, msl=msl, sp=sp: e.matmul(
                            SPA[sp][:, ksub * 512:(ksub + 1) * 512],
                            c_ident[:, :], A(MS[msl] + ksub * 512, 512),
                            start=False, stop=True),
                            wait=([(sml[msl], mtok)] if (ksub == 0 and h == 0)
                                  else []),
                            inc=(sT if ksub == 1 else None))
                sctok = sT.n
                P.s(lambda e, ptb=ptb, sp=sp: e.activation(
                    A(ptb, 1024), SPA[sp][:, 0:1024], AF.Exp),
                    wait=[(sT, sctok), pT_free[sp]], inc=sS)
                exptok = sS.n
                sc_free[sp] = (sS, exptok)

                def mk_avE(h=h, sp=sp, ptb=ptb, vsl=vsl, vtok=vtok,
                           exptok=exptok, first_kc=first_kc, kpos=kpos):
                    def em():
                        ab = h % 2
                        wt2 = [(sS, exptok), avb_free[ab]]
                        if h == 0:
                            wt2.append((svl[vsl], vtok))
                            wt2.append(rs_free[0])
                        for ksub in range(2):
                            P.t(lambda e, h=h, ksub=ksub, ab=ab, vsl=vsl,
                                ptb=ptb: e.matmul(
                                    AVB2[ab][:, 0:512],
                                    AR[:, VS[vsl] + ksub * E + h * 128:
                                       VS[vsl] + ksub * E + h * 128 + 128],
                                    A(ptb + ksub * 512, 512),
                                    start=(ksub == 0), stop=(ksub == 1)),
                                wait=(wt2 if ksub == 0 else []))
                        for ksub in range(2):
                            P.t(lambda e, h=h, ksub=ksub, ptb=ptb,
                                f=(h == 0 and ksub == 0),
                                l=(h == 15 and ksub == 1): e.matmul(
                                    RSB16, c_E[:, h * 16:(h + 1) * 16],
                                    A(ptb + ksub * 512, 512),
                                    start=f, stop=l),
                                inc=(sT if ksub == 1 else None))
                        avtok = sT.n
                        pT_free[sp] = (sT, avtok)
                        if h == 15:
                            vslot_free[vsl] = (sT, avtok)
                        ew = [(sT, avtok)]
                        if kpos == 0 and h < 2:
                            ew += avt_guard
                        if kpos == 0:
                            P.v(lambda e, h=h, ab=ab: e.tensor_copy(
                                AVT_SB[:, h * 512:(h + 1) * 512], AVB2[ab][:, 0:512]),
                                wait=ew, inc=sV)
                        else:
                            P.v(lambda e, h=h, ab=ab: e.tensor_tensor(
                                AVT_SB[:, h * 512:(h + 1) * 512], AVB2[ab][:, 0:512],
                                AVT_SB[:, h * 512:(h + 1) * 512], ALU.add),
                                wait=ew, inc=sV)
                        avb_free[ab] = (sV, sV.n)
                    return em
                flushp()
                pend[0] = mk_avE()
            kslot_free[ksl] = (sT, sT.n)
            if masked:
                mslot_free[msl] = (sT, sT.n)
            flushp()
            lastE = sT.n
            if kpos == 0:
                P.v(lambda e: e.tensor_copy(rs16[:, :], RSB16),
                    wait=[(sT, lastE)], inc=sV)
            else:
                P.v(lambda e: e.tensor_tensor(rs16[:, :], RSB16, rs16[:, :],
                                              ALU.add),
                    wait=[(sT, lastE)], inc=sV)
            rs_free[0] = (sV, sV.n)
        P.v(lambda e: e.drain())
        P.v(lambda e: e.reciprocal(rs16[:, :], rs16[:, :]))
        P.v(lambda e: e.drain())
        P.v(lambda e: e.tensor_copy(AR[0:16, ROWB:ROWB + 512], rs16[:, :]), inc=sV)
        rectok = sV.n
        rs_free[0] = (sV, rectok)
        cp_toks = {}
        for h in range(16):
            scb = h % 2
            bw = ([sc_free[0], sc_free[1], (sV, rectok)] if h < 2
                  else [(sS, cp_toks[h - 2])])
            P.t(lambda e, h=h, scb=scb: e.matmul(
                psA[:, scb * 512:(scb + 1) * 512],
                c_sel16[:, h * 128:(h + 1) * 128], AR[0:16, ROWB:ROWB + 512],
                start=True, stop=True),
                wait=bw, inc=sT)
            bctok = sT.n
            P.s(lambda e, scb=scb: e.copy(
                rbsb[:, scb * 512:(scb + 1) * 512],
                psA[:, scb * 512:(scb + 1) * 512]),
                wait=[(sT, bctok), rb_free[scb]], inc=sS)
            cp_toks[h] = sS.n
            P.v(lambda e, h=h, lq=lq, scb=scb: e.tensor_tensor(
                A(attn_off + h * NTOK + lq * 512, 512),
                AVT_SB[:, h * 512:(h + 1) * 512],
                rbsb[:, scb * 512:(scb + 1) * 512], ALU.mult),
                wait=[(sS, cp_toks[h])], inc=sV)
            rb_free[scb] = (sV, sV.n)
        sc_free[0] = (sS, sS.n)
        sc_free[1] = (sS, sS.n)
        avb_free = [(sV, sV.n), (sV, sV.n)]
    if "cck" in tap_dr:
        P.op("sync", lambda e: e.dma_start(
            out=tap_dr["cck"].ap(), in_=cc_outk.ap()),
            wait=[(sCC, 1)], inc=sOut, inc_amt=16)
    if "ccv" in tap_dr:
        P.op("sync", lambda e: e.dma_start(
            out=tap_dr["ccv"].ap(), in_=cc_outv.ap()),
            wait=[(sCC2, 1)], inc=sOut, inc_amt=16)
    attn_mm_tok = sT.n
    attn_tok = sV.n
    PAIRF[0] = (sS, sS.n)
    PAIRF[1] = (sV, attn_tok)
    PAIRF[2] = (sV, attn_tok)
    PAIRF[3] = (sV, attn_tok)
    tap("attn", attn_off, FC * NTOK)

    aob_off = R2
    first_aob = [True]
    sq_free2 = [(sT, sT.n), (sT, sT.n)]
    sum_guard = [(sV, attn_tok)]
    for oc in range(FC):
        k, wtok = w_load(lambda o=oc: dr["Wout_s"].ap()[o], FC)
        pi = oc % 2
        mm_block(k, wtok, FC,
                 lambda ic, th: A(attn_off + ic * NTOK + th * 512, 512), pi,
                 extra_wait=([(sV, attn_tok)] if oc == 0 else []))
        mmtok = sT.n
        for th in range(2):
            wt = [(sT, mmtok)]
            if first_aob[0]:
                wt.append((sT, attn_mm_tok))
                first_aob[0] = False
            P.v(lambda e, oc=oc, th=th, pi=pi: e.tensor_scalar(
                A(aob_off + oc * NTOK + th * 512, 512), PB[pi][th],
                c_bout[:, oc:oc + 1], None, ALU.add),
                wait=(wt if th == 0 else []),
                inc=(sV if th == 1 else None))
        evtok = sV.n
        PAIRF[pi] = (sV, evtok)
        P.s(lambda e, oc=oc: e.activation(
            A(SQREG + 0 * NTOK, NTOK), A(aob_off + oc * NTOK, NTOK),
            AF.Square),
            wait=[(sV, evtok), sq_free2[oc % 2]], inc=sS)
        sqtok = sS.n
        for th in range(2):
            P.t(lambda e, oc=oc, th=th: e.matmul(
                psD[0:1, th * 512:(th + 1) * 512], c_ocol[:, :],
                A(aob_off + oc * NTOK + th * 512, 512),
                start=(oc == 0), stop=(oc == FC - 1)),
                wait=([(sV, evtok)] + (sum_guard if oc == 0 else []))
                if th == 0 else [])
            P.t(lambda e, oc=oc, th=th: e.matmul(
                PB[2][th][0:1, :], c_ocol[:, :],
                A(SQREG + 0 * NTOK + th * 512, 512),
                start=(oc == 0), stop=(oc == FC - 1)),
                wait=([(sS, sqtok)] + ([PAIRF[2]] if oc == 0 else []))
                if th == 0 else [],
                inc=(sT if th == 1 else None))
        sq_free2[oc % 2] = (sT, sT.n)
    wout_mm_tok = sT.n
    rtok = ln_rows((sT, wout_mm_tok))
    btok = ln_bcast(rtok, 0, 1)
    x1b_off = R1
    first_x1b = [True]
    ot_free = [[(sV, sV.n)], [(sV, sV.n)]]
    sq_free3 = [(sT, sT.n), (sT, sT.n)]
    for ic in range(FC):
        sl = ic % 2
        P.op("sync", lambda e, sl=sl, ic=ic: e.dma_start(
            out=xt[sl][:, :], in_=dr["x_fm"].ap()[ic]),
            wait=xt_free[sl], inc=sxl[sl], inc_amt=16)
        ltok = sxl[sl].n
        for th in range(2):
            P.v(lambda e, ic=ic, th=th: e.tensor_tensor(
                t1s[th][:, :], A(aob_off + ic * NTOK + th * 512, 512), PB[0][th],
                ALU.mult),
                wait=([(sT, btok)] if (ic == 0 and th == 0) else []))
            P.v(lambda e, th=th: e.tensor_tensor(
                t1s[th][:, :], t1s[th][:, :], PB[1][th], ALU.subtract))
            wt = [(sxl[sl], ltok)] + (ot_free[sl] if th == 0 else [])
            P.v(lambda e, sl=sl, th=th: e.tensor_tensor(
                ot[sl][:, th * 512:(th + 1) * 512],
                xt[sl][:, th * 512:(th + 1) * 512],
                t1s[th][:, :], ALU.add),
                wait=wt, inc=(sV if th == 1 else None))
        x1tok = sV.n
        xt_free[sl] = [(sV, x1tok)]
        P.op("sync", lambda e, sl=sl, ic=ic: e.dma_start(
            out=x1_hbm.ap()[ic], in_=ot[sl][:, :]),
            wait=[(sV, x1tok)], inc=sXw, inc_amt=16)
        xwtok = sXw.n
        wt = [(sV, x1tok)]
        if first_x1b[0]:
            wt.append((sT, wout_mm_tok))
            first_x1b[0] = False
        P.v(lambda e, ic=ic, sl=sl: e.tensor_copy(
            A(x1b_off + ic * NTOK, NTOK), ot[sl][:, :]), wait=wt, inc=sV)
        cptok = sV.n
        P.s(lambda e, ic=ic, sl=sl: e.activation(
            A(SQREG + 0 * NTOK, NTOK), ot[sl][:, :], AF.Square),
            wait=[(sV, x1tok), sq_free3[ic % 2]], inc=sS)
        sqtok = sS.n
        ot_free[sl] = [(sXw, xwtok), (sV, cptok), (sS, sqtok)]
        for th in range(2):
            P.t(lambda e, ic=ic, th=th: e.matmul(
                psD[0:1, th * 512:(th + 1) * 512], c_ocol[:, :],
                A(x1b_off + ic * NTOK + th * 512, 512),
                start=(ic == 0), stop=(ic == FC - 1)),
                wait=([(sV, cptok)] + ([(sV, rtok)] if ic == 0 else []))
                if th == 0 else [])
            P.t(lambda e, ic=ic, th=th: e.matmul(
                PB[2][th][0:1, :], c_ocol[:, :],
                A(SQREG + 0 * NTOK + th * 512, 512),
                start=(ic == 0), stop=(ic == FC - 1)),
                wait=([(sS, sqtok)] + ([(sV, rtok)] if ic == 0 else []))
                if th == 0 else [],
                inc=(sT if th == 1 else None))
        sq_free3[ic % 2] = (sT, sT.n)
    ln2_stats_tok = sT.n
    sandwich_v_tok = sV.n
    tap("x1b", x1b_off, FC * NTOK)
    rtok = ln_rows((sT, ln2_stats_tok))
    btok = ln_bcast(rtok, 0, 1)
    h2_off = R0
    first_h2 = [True]
    for ic in range(FC):
        for th in range(2):
            wt = []
            if first_h2[0]:
                wt = [(sT, btok), (sT, qkv_mm_tok), (sV, sandwich_v_tok)]
                first_h2[0] = False
            P.v(lambda e, ic=ic, th=th: e.tensor_tensor(
                A(h2_off + ic * NTOK + th * 512, 512),
                A(x1b_off + ic * NTOK + th * 512, 512), PB[0][th], ALU.mult),
                wait=wt)
            P.v(lambda e, ic=ic, th=th: e.tensor_tensor(
                A(h2_off + ic * NTOK + th * 512, 512),
                A(h2_off + ic * NTOK + th * 512, 512), PB[1][th], ALU.subtract),
                inc=(sV if (ic == FC - 1 and th == 1) else None))
    h2_tok = sV.n
    PAIRF[0] = (sV, h2_tok)
    PAIRF[1] = (sV, h2_tok)
    tap("h2", h2_off, FC * NTOK)

    gfree = [PAIRF[3], PAIRF[3]]
    gate_defer = []
    cb_last = [0]
    for tc in range(8):
        gb = tc % 2
        for ic in range(FC):
            P.t(lambda e, tc=tc, ic=ic, gb=gb: e.matmul(
                psD[:, gb * 512:gb * 512 + 8],
                A(h2_off + ic * NTOK + tc * 128, 128),
                c_gate[:, ic * 8:(ic + 1) * 8],
                start=(ic == 0), stop=(ic == FC - 1)),
                wait=([gfree[gb], (sV, h2_tok), CSTG] if ic == 0 else []),
                inc=(sT if ic == FC - 1 else None))
        gmtok = sT.n
        L = psD[:, gb * 512:gb * 512 + 8]
        P.v(lambda e, L=L: e.reduce_max(pp[:, 0:1], L, AX.X), wait=[(sT, gmtok)])
        P.v(lambda e: e.drain())
        P.v(lambda e: e.tensor_scalar(pp[:, 1:2], pp[:, 0:1], -1.0, None, ALU.mult),
            inc=sV)
        vt = sV.n
        P.s(lambda e, L=L: e.activation(gsc[:, 0:8], L, AF.Exp, bias=pp[:, 1:2]),
            wait=[(sV, vt)], inc=sS)
        st = sS.n
        gfree[gb] = (sS, st)
        P.v(lambda e: e.reduce_sum(pp[:, 2:3], gsc[:, 0:8], AX.X), wait=[(sS, st)])
        P.v(lambda e: e.drain())
        P.v(lambda e: e.reciprocal(pp[:, 3:4], pp[:, 2:3]))
        P.v(lambda e: e.drain())
        P.v(lambda e, tc=tc: e.tensor_scalar(
            gpool[:, tc * 8:(tc + 1) * 8], gsc[:, 0:8], pp[:, 3:4], None, ALU.mult))
        P.v(lambda e: e.drain())
        P.v(lambda e, tc=tc: e.reduce_max(pp[:, 4:5], gpool[:, tc * 8:(tc + 1) * 8],
                                          AX.X))
        P.v(lambda e: e.drain())
        P.v(lambda e, tc=tc: e.tensor_scalar(
            gsc[:, 8:16], gpool[:, tc * 8:(tc + 1) * 8], pp[:, 4:5], None, ALU.is_ge))
        P.v(lambda e: e.drain())
        P.v(lambda e, tc=tc: e.scalar_tensor_tensor(
            gsc[:, 16:24], gsc[:, 8:16], -2.0, gpool[:, tc * 8:(tc + 1) * 8],
            ALU.mult, ALU.add))
        P.v(lambda e: e.drain())
        P.v(lambda e: e.reduce_max(pp[:, 5:6], gsc[:, 16:24], AX.X))
        P.v(lambda e: e.drain())
        P.v(lambda e, tc=tc: e.tensor_scalar(
            gsc[:, 8:16], gpool[:, tc * 8:(tc + 1) * 8], pp[:, 5:6], None, ALU.is_ge))
        P.v(lambda e: e.drain())
        P.v(lambda e, tc=tc: e.tensor_tensor(
            gpool[:, tc * 8:(tc + 1) * 8], gpool[:, tc * 8:(tc + 1) * 8],
            gsc[:, 8:16], ALU.mult))
        P.v(lambda e: e.drain())
        P.v(lambda e, tc=tc: e.tensor_copy(
            AR[:, PTREG + tc * 8:PTREG + (tc + 1) * 8],
            gpool[:, tc * 8:(tc + 1) * 8]),
            inc=sV)
        cbtok = sV.n
        def mk_tp(tc=tc, gb=gb, cbtok=cbtok):
            def em():
                P.t(lambda e, tc=tc, gb=gb: e.transpose(
                    psD.bitcast(BF16)[0:8, gb * 1024:gb * 1024 + 128],
                    AR[:, PTREG + tc * 8:PTREG + (tc + 1) * 8], c_ident[:, :]),
                    wait=[(sV, cb_last[0]), tfree[gb]], inc=sT)
                tttok = sT.n
                P.v(lambda e, tc=tc, gb=gb: e.tensor_copy(
                    AR[0:8, ROWB + tc * 128:ROWB + (tc + 1) * 128],
                    psD.bitcast(BF16)[0:8, gb * 1024:gb * 1024 + 128]),
                    wait=[(sT, tttok)], inc=sV)
                tfree[gb] = (sV, sV.n)
            return em
        gate_defer.append(mk_tp())
    gate_exp_tok = sS.n
    tfree = [(sS, gate_exp_tok), (sS, gate_exp_tok)]
    cb_last[0] = sV.n

    def hs_off(ic):
        return (R2 + ic * NTOK) if ic < 16 else (POOL + (ic - 16) * NTOK)

    y_off = R1
    mreg_free = (sT, attn_mm_tok)
    first_hs = [True]
    comb_tok = None
    for oc in range(ICS):
        if oc == 3:
            for f in gate_defer:
                f()
            comb_tok = sV.n
            PAIRF[3] = (sV, comb_tok)
            if "gpool" in tap_dr:
                P.op("sync", lambda e: e.dma_start(
                    out=tap_dr["gpool"].ap(), in_=gpool[:, :]),
                    wait=[(sV, comb_tok)], inc=sOut, inc_amt=16)
            if "comb" in tap_dr:
                P.op("sync", lambda e: e.dma_start(
                    out=tap_dr["comb"].ap()[0:8, 0:1024],
                    in_=AR[0:8, ROWB:ROWB + 1024]),
                    wait=[(sV, comb_tok)], inc=sOut, inc_amt=16)
        k1, wtok1 = w_load(lambda o=oc: dr["SW1_s"].ap()[o], FC)
        mm_block(k1, wtok1, FC,
                 lambda ic, th: A(h2_off + ic * NTOK + th * 512, 512), 0,
                 extra_wait=([(sV, h2_tok)] if oc == 0 else []))
        g1tok = sT.n
        k2, wtok2 = w_load(lambda o=oc: dr["SW2_s"].ap()[o], FC)
        mm_block(k2, wtok2, FC,
                 lambda ic, th: A(h2_off + ic * NTOK + th * 512, 512), 1)
        g2tok = sT.n
        for th in range(2):
            P.s(lambda e, th=th: e.activation(
                A(MREG + th * 512, 512), PB[0][th], AF.Silu),
                wait=([(sT, g1tok), mreg_free] if th == 0 else []),
                inc=(sS if th == 1 else None))
        silu_tok = sS.n
        PAIRF[0] = (sS, silu_tok)
        for th in range(2):
            wt = [(sT, g2tok), (sS, silu_tok)]
            if first_hs[0]:
                wt.append((sT, attn_mm_tok))
                first_hs[0] = False
            P.v(lambda e, oc=oc, th=th: e.tensor_tensor(
                A(hs_off(oc) + th * 512, 512), PB[1][th],
                A(MREG + th * 512, 512), ALU.mult),
                wait=(wt if th == 0 else []),
                inc=(sV if th == 1 else None))
        PAIRF[1] = (sV, sV.n)
        mreg_free = (sV, sV.n)
    hs_tok = sV.n
    first_y = [True]
    for oc in range(FC):
        k3, wtok3 = w_load(lambda o=oc: dr["SW3_s"].ap()[o], ICS)
        pe = 2 + oc % 2
        mm_block(k3, wtok3, ICS, lambda ic, th: A(hs_off(ic) + th * 512, 512), pe,
                 extra_wait=([(sV, hs_tok)] if oc == 0 else []))
        shtok = sT.n
        for th in range(2):
            wt = [(sT, shtok)]
            if first_y[0]:
                wt.append((sV, hs_tok))
                first_y[0] = False
            P.v(lambda e, oc=oc, th=th, pe=pe: e.tensor_copy(
                A(y_off + oc * NTOK + th * 512, 512), PB[pe][th]),
                wait=(wt if th == 0 else []),
                inc=(sV if th == 1 else None))
        PAIRF[pe] = (sV, sV.n)
    sw3_tok = sT.n

    def cb_off(e):
        return (R2 + 11264 + e * NTOK) if e < 5 else (POOL + 11264 + (e - 5) * NTOK)

    first_cb = [True]
    for e_ in range(8):
        for th in range(2):
            wt = [PAIRF[1]] if th == 0 else []
            if first_cb[0]:
                wt.append((sT, sw3_tok))
                wt.append((sV, comb_tok))
                first_cb[0] = False
            P.t(lambda ee, e_=e_, th=th: ee.matmul(
                PB[1][th], c_sel8[:, e_ * 128:(e_ + 1) * 128],
                AR[0:8, ROWB + th * 512:ROWB + (th + 1) * 512],
                start=True, stop=True),
                wait=wt, inc=(sT if th == 1 else None))
        bctok = sT.n
        for th in range(2):
            P.v(lambda ee, e_=e_, th=th: ee.tensor_copy(
                AR[:, cb_off(e_) + th * 512:cb_off(e_) + (th + 1) * 512],
                PB[1][th]),
                wait=([(sT, bctok)] if th == 0 else []),
                inc=(sV if th == 1 else None))
        PAIRF[1] = (sV, sV.n)
    cb_tok = sV.n

    he_free = [(sT, sw3_tok), (sT, sw3_tok)]
    for e_ in range(8):
        heb = POOL if e_ % 2 else R2
        for oc in range(ICM):
            k1, wtok1 = w_load(lambda e2=e_, o=oc: dr["W1_s"].ap()[e2, o], FC)
            mm_block(k1, wtok1, FC,
                     lambda ic, th: A(h2_off + ic * NTOK + th * 512, 512), 0)
            g1tok = sT.n
            k2, wtok2 = w_load(lambda e2=e_, o=oc: dr["W2_s"].ap()[e2, o], FC)
            mm_block(k2, wtok2, FC,
                     lambda ic, th: A(h2_off + ic * NTOK + th * 512, 512), 1)
            g2tok = sT.n
            for th in range(2):
                P.s(lambda e, th=th: e.activation(
                    A(MREG + th * 512, 512), PB[0][th], AF.Silu),
                    wait=([(sT, g1tok), mreg_free] if th == 0 else []),
                    inc=(sS if th == 1 else None))
            silu_tok = sS.n
            PAIRF[0] = (sS, silu_tok)
            for th in range(2):
                P.v(lambda e, e_=e_, th=th: e.tensor_tensor(
                    A(MREG + 1024 + th * 512, 512), PB[1][th],
                    AR[:, cb_off(e_) + th * 512:cb_off(e_) + (th + 1) * 512],
                    ALU.mult),
                    wait=([(sT, g2tok)] if th == 0 else []))
                wt = [(sS, silu_tok)]
                if oc == 0 and th == 0:
                    wt.append(he_free[e_ % 2])
                P.v(lambda e, oc=oc, th=th, heb=heb: e.tensor_tensor(
                    A(heb + oc * NTOK + th * 512, 512),
                    A(MREG + 1024 + th * 512, 512),
                    A(MREG + th * 512, 512), ALU.mult),
                    wait=wt, inc=(sV if th == 1 else None))
            PAIRF[1] = (sV, sV.n)
            mreg_free = (sV, sV.n)
        he_tok = sV.n
        for oc in range(FC):
            k3, wtok3 = w_load(lambda e2=e_, o=oc: dr["W3_s"].ap()[e2, o], ICM)
            pe = 2 + oc % 2
            mm_block(k3, wtok3, ICM,
                     lambda ic, th, heb=heb: A(heb + ic * NTOK + th * 512, 512), pe,
                     extra_wait=([(sV, he_tok)] if oc == 0 else []))
            eotok = sT.n
            for th in range(2):
                P.v(lambda e, oc=oc, th=th, pe=pe: e.tensor_tensor(
                    A(y_off + oc * NTOK + th * 512, 512), PB[pe][th],
                    A(y_off + oc * NTOK + th * 512, 512), ALU.add),
                    wait=([(sT, eotok)] if th == 0 else []),
                    inc=(sV if th == 1 else None))
            PAIRF[pe] = (sV, sV.n)
        he_free[e_ % 2] = (sT, sT.n)
    y_tok = sV.n
    ffn_mm_tok = sT.n
    tap("y", y_off, FC * NTOK)

    sq_free4 = [(sT, sT.n), (sT, sT.n)]
    for ic in range(FC):
        P.s(lambda e, ic=ic: e.activation(
            A(SQREG + 0 * NTOK, NTOK), A(y_off + ic * NTOK, NTOK), AF.Square),
            wait=[(sV, y_tok), sq_free4[ic % 2]], inc=sS)
        sqtok = sS.n
        for th in range(2):
            P.t(lambda e, ic=ic, th=th: e.matmul(
                psD[0:1, th * 512:(th + 1) * 512], c_ocol[:, :],
                A(y_off + ic * NTOK + th * 512, 512),
                start=(ic == 0), stop=(ic == FC - 1)),
                wait=([(sV, y_tok), PAIRF[3]] if (ic == 0 and th == 0) else []))
            P.t(lambda e, ic=ic, th=th: e.matmul(
                PB[2][th][0:1, :], c_ocol[:, :],
                A(SQREG + 0 * NTOK + th * 512, 512),
                start=(ic == 0), stop=(ic == FC - 1)),
                wait=([(sS, sqtok)] + ([PAIRF[2]] if ic == 0 else []))
                if th == 0 else [],
                inc=(sT if th == 1 else None))
        sq_free4[ic % 2] = (sT, sT.n)
    ln3_stats_tok = sT.n
    rtok = ln_rows((sT, ln3_stats_tok))
    btok = ln_bcast(rtok, 0, 1)
    for ic in range(FC):
        sl = ic % 2
        P.op("scalar", lambda e, sl=sl, ic=ic: e.dma_start(
            out=xt[sl][:, :], in_=x1_hbm.ap()[ic]),
            wait=xt_free[sl] + ([(sXw, sXw.n)] if ic == 0 else []),
            inc=sxl[sl], inc_amt=16)
        ltok = sxl[sl].n
        for th in range(2):
            P.v(lambda e, ic=ic, th=th: e.tensor_tensor(
                t1s[th][:, :], A(y_off + ic * NTOK + th * 512, 512), PB[0][th],
                ALU.mult),
                wait=([(sT, btok)] if (ic == 0 and th == 0) else []))
            P.v(lambda e, th=th: e.tensor_tensor(
                t1s[th][:, :], t1s[th][:, :], PB[1][th], ALU.subtract))
            wt = [(sxl[sl], ltok)] + (ot_free[sl] if th == 0 else [])
            P.v(lambda e, sl=sl, th=th: e.tensor_tensor(
                ot[sl][:, th * 512:(th + 1) * 512],
                xt[sl][:, th * 512:(th + 1) * 512], t1s[th][:, :], ALU.add),
                wait=wt, inc=(sV if th == 1 else None))
        otok = sV.n
        xt_free[sl] = [(sV, otok)]
        P.op("sync", lambda e, sl=sl, ic=ic: e.dma_start(
            out=out_fm.ap()[ic], in_=ot[sl][:, :]),
            wait=[(sV, otok)], inc=sOut, inc_amt=16)
        ot_free[sl] = [(sOut, sOut.n)]

    fin = sOut.n
    P.op("sync", lambda e: e.nop(), wait=[(sOut, fin)])

    global _LAST_Q
    _LAST_Q = {e: len(P.q[e]) for e in P.ENGINES}
    with nc.Block() as block:
        @block.tensor
        def _(e):
            for f in P.q["tensor"]:
                f(e)

        @block.vector
        def _(e):
            for f in P.q["vector"]:
                f(e)

        @block.scalar
        def _(e):
            for f in P.q["scalar"]:
                f(e)

        @block.gpsimd
        def _(e):
            for f in P.q["gpsimd"]:
                f(e)

        @block.sync
        def _(e):
            for f in P.q["sync"]:
                f(e)

    es.close()
    return nc


def kernel(**inputs):
    sh, pc = _prep_host(inputs)
    nc = build_nc()
    in_maps = [dict(sh, **pc[c]) for c in range(NCORE)]
    res = run_bass_kernel_spmd(nc, in_maps, core_ids=list(range(NCORE)))
    return _assemble_out(res.results)
